# version 18
# baseline (speedup 1.0000x reference)
import contextlib
import numpy as np
import ml_dtypes
import concourse.bass as bass
import concourse.mybir as mybir
from concourse.bass_utils import run_bass_kernel_spmd

F32 = mybir.dt.float32
BF16 = mybir.dt.bfloat16
AF = mybir.ActivationFunctionType
ALU = mybir.AluOpType
AX = mybir.AxisListType

D = 1024
TOK = 2048
NH = 8
SUB = 4096
NCH = 4
CH = 512
SCALE = float(1.0 / np.sqrt(2.0 * D))
EPS = 1e-5
ZC = float(2.0 ** 20)
MC = 512
MW = 1152


class _MockInstr:
    def then_inc(self, sem, k=1, **kw):
        return self

    def annotate(self, *a, **kw):
        return self


class _Mock:
    def __getattr__(self, name):
        if name == "wait_ge":
            return lambda *a, **kw: None
        return lambda *a, **kw: _MockInstr()


class Builder:
    def __init__(self, n_cores=8, debug=False):
        self.n_cores = n_cores
        self.debug = debug
        self.nc = bass.Bass("TRN2", target_bir_lowering=False,
                            detect_race_conditions=False)
        self.cnt = {}
        self.sem = {}
        self.dbg_list = []
        self.prog = []
        self.pass2 = False

    def mksem(self, name):
        s = self.es.enter_context(self.nc.semaphore(name))
        self.sem[name] = s
        self.cnt[name] = 0

    def inc(self, instr, name, k=1):
        instr.then_inc(self.sem[name], k)
        self.cnt[name] += k
        return self.cnt[name]

    def wait(self, eng, name, val):
        if not self.pass2:
            assert val <= self.cnt[name], (name, val, self.cnt[name])
        if val > 0:
            eng.wait_ge(self.sem[name], val)

    def sb(self, name, shape, dtype):
        return self.es.enter_context(self.nc.sbuf_tensor("sb_" + name, shape, dtype))

    def ps(self, name, shape, dtype=F32):
        return self.es.enter_context(self.nc.psum_tensor(name, shape, dtype))

    def T(self, f): self._rec("tensor", f)
    def V(self, f): self._rec("vector", f)
    def S(self, f): self._rec("scalar", f)
    def G(self, f): self._rec("gpsimd", f)
    def Y(self, f): self._rec("sync", f)

    def _rec(self, eng, f):
        self.prog.append((eng, f))
        f(_Mock())

    def dbg(self, name, ap, shape, dtype=F32):
        if not self.debug:
            return
        t = self.nc.dram_tensor("dbg_" + name, list(shape), dtype,
                                kind="ExternalOutput").ap()
        self.dbg_list.append(name)
        w = {k: self.cnt[k] for k in ("v", "t", "s", "g")}

        def f(sy):
            for k, vv in w.items():
                self.wait(sy, k, vv)
            self.inc(sy.dma_start(out=t, in_=ap), "st", 16)
        self.Y(f)

    def build(self):
        nc = self.nc
        dr = {}
        dr["x"] = nc.dram_tensor("x", [8, 128, TOK], F32, kind="ExternalInput").ap()
        dr["wqkv"] = nc.dram_tensor("wqkv", [8, 128, 3072], F32, kind="ExternalInput").ap()
        dr["biasp"] = nc.dram_tensor("biasp", [128, 24], F32, kind="ExternalInput").ap()
        for nm in ["wq_t", "wk_t", "winv_t", "rw1_t", "wo_t"]:
            dr[nm] = nc.dram_tensor(nm, [8, 128, 1024], F32, kind="ExternalInput").ap()
        dr["rw2_t"] = nc.dram_tensor("rw2_t", [8, 128, 8], F32, kind="ExternalInput").ap()
        for nm in ["vf_r", "wf_r", "wie_r", "vi_r"]:
            dr[nm] = nc.dram_tensor(nm, [8, 128, 512], F32, kind="ExternalInput").ap()
        for nm in ["uf_r", "ui_r"]:
            dr[nm] = nc.dram_tensor(nm, [4, 128, 1024], F32, kind="ExternalInput").ap()
        dr["b_fwd"] = nc.dram_tensor("b_fwd", [8, 1024], F32, kind="ExternalInput").ap()
        dr["b_inv"] = nc.dram_tensor("b_inv", [8, 1024], F32, kind="ExternalInput").ap()
        dr["rb1"] = nc.dram_tensor("rb1", [128, 8], F32, kind="ExternalInput").ap()
        dr["b2"] = nc.dram_tensor("b2", [8, 1], F32, kind="ExternalInput").ap()
        dr["eb"] = nc.dram_tensor("eb", [8, 1], F32, kind="ExternalInput").ap()
        dr["alpha"] = nc.dram_tensor("alpha", [1, 1], F32, kind="ExternalInput").ap()
        dr["ident"] = nc.dram_tensor("ident", [128, 128], F32, kind="ExternalInput").ap()
        dr["mmask"] = nc.dram_tensor("mmask", [128, MW], BF16, kind="ExternalInput").ap()
        dr["sel"] = nc.dram_tensor("sel", [8, 512], BF16, kind="ExternalInput").ap()
        dr["ninv"] = nc.dram_tensor("ninv", [1, TOK], F32, kind="ExternalInput").ap()
        dr["mskb"] = nc.dram_tensor("mskb", [2, 1], F32, kind="ExternalInput").ap()
        dr["y"] = nc.dram_tensor("y", [8, 128, TOK], F32, kind="ExternalOutput").ap()
        self.cc_in = nc.dram_tensor("cc_in", [1, 1024], F32)
        self.cc_out = nc.dram_tensor("cc_out", [2, 1024], F32)
        self.xmid_d = nc.dram_tensor("xmid_d", [8, 128, TOK], F32)
        self.dr = dr

        with contextlib.ExitStack() as es:
            self.es = es
            for s in (["ld", "lg", "st", "cc", "t", "v", "s", "g", "du0", "du1", "lgc"]
                      + [f"lb{i}" for i in range(3)] + [f"ls{i}" for i in range(4)]
                      + ["xsp", "xc0", "xc1", "lf", "ccd"]):
                self.mksem(s)
            self._persistent()
            self._phase0()
            with contextlib.ExitStack() as attn_es:
                outer = self.es
                self.es = attn_es
                self._attn_alloc()
                self.es = outer
                for hp in range(NH // 2):
                    self._attn_pair(hp)
                self.dbg("xmid", self.X[:], (128, 8 * TOK), F32)
            with contextlib.ExitStack() as ffn_es:
                outer = self.es
                self.es = ffn_es
                self._ffn_alloc()
                self.es = outer
                self._ffn_ln3_all()
                self.dbg("sk", self.sK[:], (128, 8), F32)
                self._ffn_exchange()
                rdy = self._xc_load(0, 0)
                ln3v, _ = self._ln_chunk(self.xc[0][:], rdy, self.ln3c,
                                         banks=(2, 3))
                for c in range(NCH):
                    ln3v, rdy = self._ffn_chunk(c, ln3v, rdy)
            self._finish()
            self.pass2 = True
            self.cnt = {k: 0 for k in self.cnt}
            with nc.Block() as block:
                for eng in ["tensor", "scalar", "vector", "gpsimd", "sync"]:
                    fs = [f for e, f in self.prog if e == eng]
                    if not fs:
                        continue
                    getattr(block, eng)(lambda handle, fs=fs: [f(handle) for f in fs])
        return nc

    def _persistent(self):
        self.ident = self.sb("ident", [128, 128], F32)
        self.ones_bf = self.sb("ones_bf", [128, 128], BF16)
        self.ones_row = self.sb("ones_row", [1, 128], F32)
        self.negsc = self.sb("negsc", [1, 128], F32)
        self.ones_col = self.sb("ones_col", [128, 1], F32)
        self.neg8 = self.sb("neg8", [1, 8], F32)
        self.alpha = self.sb("alpha", [1, 1], F32)
        self.biasp = self.sb("biasp", [128, 24], F32)
        self.qsc = self.sb("qsc", [128, 1], F32)
        self.epsc = self.sb("epsc", [1, 1], F32)
        self.mrow = self.sb("mrow", [1, 512], F32)
        self.vrow = self.sb("vrow", [1, 512], F32)
        self.rrow = self.sb("rrow", [1, 512], F32)
        self.zirow = self.sb("zirow", [1, 512], F32)

    def _phase0(self):
        dr = self.dr

        def fy(sy):
            self.inc(sy.dma_start(out=self.ident[:], in_=dr["ident"]), "ld", 16)
            self.inc(sy.dma_start(out=self.biasp[:], in_=dr["biasp"]), "ld", 16)
            self.inc(sy.dma_start(out=self.alpha[:], in_=dr["alpha"]), "ld", 16)
        self.Y(fy)

        def fv(v):
            v.memset(self.ones_bf[:], 1.0)
            v.memset(self.ones_row[:], 1.0)
            v.memset(self.negsc[:], -1.05)
            v.memset(self.ones_col[:], 1.0)
            v.memset(self.neg8[:], -1.0)
            v.memset(self.qsc[0:64, :], SCALE)
            v.memset(self.epsc[:], EPS)
            self.inc(v.memset(self.qsc[64:128, :], 1.0), "v")
        self.V(fv)
        self.ld_total = self.cnt["ld"]

    def _attn_alloc(self):
        self.X = self.sb("X", [128, 8 * TOK], F32)
        self.mmask = self.sb("mmask", [128, MW], BF16)
        self.wqkv_sb = self.sb("wqkv_sb", [128, 8 * 3072], BF16)
        self.qk_lo = [self.sb(f"qk_lo{i}", [128, SUB], BF16) for i in range(2)]
        qkh = self.sb("qk_hi", [128, SUB], BF16)
        self.qk_hi = [qkh, qkh]
        self.g_qkhi = ("t", 0)
        self.v2 = [self.sb(f"v2{i}", [128, 2048], F32) for i in range(2)]
        va = self.sb("vaug", [128, 32 * 128], BF16)
        self.vaug = [va, va]
        self.ln1x = self.sb("ln1x", [128, 8 * 512], BF16)
        self.xbf = self.sb("xbf", [128, 8 * 512], BF16)
        self.x2bf = self.sb("x2bf", [128, 8 * 512], BF16)
        self.attn_bf = self.sb("attn_bf", [128, 8 * 256], BF16)
        self.etile = [self.sb(f"etile{i}", [128, 1024], BF16) for i in range(2)]
        self.negM = self.sb("negM", [128, 2], F32)
        self.oraw = self.sb("oraw", [128, 1024], BF16)
        self.lnt = self.x2bf[0:128, 0:512].bitcast(F32)
        self.red = self.sb("red", [64, 8], F32)
        self.P_qkv = [self.ps(f"P_qkv{i}", [128, 512], F32) for i in range(2)]
        self.P_S = [self.ps(f"P_S{i}", [128, 1024], F32) for i in range(2)]
        self.P_O = self.ps("P_O", [128, 512], F32)
        self.P_misc = self.ps("P_misc", [128, 512], F32)
        self.g_qkv = [("v", 0), ("v", 0)]
        self.g_S = [("s", 0), ("s", 0)]
        self.g_etile = [("t", 0), ("t", 0)]
        self.g_O = ("v", 0)
        self.g_misc = ("v", 0)
        self.g_negM = ("s", 0)
        self.g_oraw = [("v", 0), ("v", 0)]
        self.mid_emit = []

        def fy0(sy):
            self.inc(sy.dma_start(out=self.X[:].rearrange("p (j t) -> p j t", j=8),
                                  in_=self.dr["x"].rearrange("j p t -> p j t")), "ld", 16)
            self.inc(sy.dma_start(out=self.mmask[:], in_=self.dr["mmask"]), "ld", 16)
        self.Y(fy0)
        self.ld_total = self.cnt["ld"]

        def fg(g):
            self.inc(g.dma_start(out=self.wqkv_sb[:].rearrange("p (j f) -> p j f", j=8),
                                 in_=self.dr["wqkv"].rearrange("j p f -> p j f")), "lg", 16)
        self.G(fg)
        self.lg0_total = self.cnt["lg"]

    def _ln_rows(self, ncols):
        def fv(v):
            v.tensor_mul(self.zirow[0:1, 0:ncols], self.mrow[0:1, 0:ncols], self.mrow[0:1, 0:ncols])
            self.inc(v.tensor_sub(self.vrow[0:1, 0:ncols], self.vrow[0:1, 0:ncols],
                                  self.zirow[0:1, 0:ncols]), "v")
        self.V(fv)
        vw = self.cnt["v"]

        def fs(s):
            self.wait(s, "v", vw)
            s.activation(self.vrow[0:1, 0:ncols], self.vrow[0:1, 0:ncols], AF.Ln, bias=self.epsc[:])
            self.inc(s.activation(self.rrow[0:1, 0:ncols], self.vrow[0:1, 0:ncols],
                                  AF.Exp, scale=-0.5), "s")
        self.S(fs)
        return self.cnt["s"]

    def _attn_pair(self, hp):
        X = self.X
        pc = hp * 512

        ldw = self.cnt["ld"]

        def fv(v):
            self.wait(v, "ld", self.ld_total)
            for j in range(8):
                xs = X[:, j * TOK + pc: j * TOK + pc + 512]
                v.tensor_copy(self.xbf[:, j * 512:(j + 1) * 512], xs)
            for j in range(8):
                ins = v.tensor_mul(self.x2bf[:, j * 512:(j + 1) * 512],
                                   self.xbf[:, j * 512:(j + 1) * 512],
                                   self.xbf[:, j * 512:(j + 1) * 512])
            self.inc(ins, "v")
        self.V(fv)
        vw = self.cnt["v"]

        psA, psB = self.P_qkv
        (sgn0, sgv0), (sgn1, sgv1) = self.g_qkv

        def ft(t, sgn0=sgn0, sgv0=sgv0, sgn1=sgn1, sgv1=sgv1, vw=vw):
            self.wait(t, "v", vw)
            self.wait(t, sgn0, sgv0)
            self.wait(t, sgn1, sgv1)
            for j in range(8):
                t.matmul(psA[:], self.ones_bf[:], self.xbf[:, j * 512:(j + 1) * 512],
                         start=(j == 0), stop=(j == 7))
            for j in range(8):
                ins = t.matmul(psB[:], self.ones_bf[:], self.x2bf[:, j * 512:(j + 1) * 512],
                               start=(j == 0), stop=(j == 7))
            self.inc(ins, "t")
        self.T(ft)
        tw = self.cnt["t"]

        def fsx(v):
            self.wait(v, "t", tw)
            v.tensor_scalar(self.mrow[0:1, 0:512], psA[0:1, 0:512], 1.0 / D, None, ALU.mult)
            self.inc(v.tensor_scalar(self.vrow[0:1, 0:512], psB[0:1, 0:512], 1.0 / D, None, ALU.mult), "v")
        self.V(fsx)
        sw = self._ln_rows(512)

        def ft2(t):
            self.wait(t, "s", sw)
            self.inc(t.matmul(psB[:, 0:512], self.ones_row[:], self.rrow[0:1, 0:512],
                              start=True, stop=True), "t")
        self.T(ft2)
        tw2 = self.cnt["t"]

        def fv2(v):
            self.wait(v, "t", tw2)
            for j in range(8):
                xs = X[:, j * TOK + pc: j * TOK + pc + 512]
                v.scalar_tensor_tensor(self.xbf[:, j * 512:(j + 1) * 512],
                                       psA[:, 0:512], -1.0 / D, xs, ALU.mult, ALU.add)
                ins = v.tensor_mul(self.ln1x[:, j * 512:(j + 1) * 512],
                                   self.xbf[:, j * 512:(j + 1) * 512], psB[:, 0:512])
            self.inc(ins, "v")
        self.V(fv2)
        vw2 = self.cnt["v"]
        self.g_qkv = [("v", vw2), ("v", vw2)]

        for fi in range(24):
            pb = fi % 2
            P = self.P_qkv[pb]
            gn, gv = self.g_qkv[pb]

            def ft3(t, fi=fi, P=P, gn=gn, gv=gv):
                self.wait(t, "lg", self.lg0_total)
                self.wait(t, "v", vw2)
                self.wait(t, gn, gv)
                for dj in range(8):
                    ins = t.matmul(P[:],
                                   self.wqkv_sb[:, dj * 3072 + fi * 128: dj * 3072 + (fi + 1) * 128],
                                   self.ln1x[:, dj * 512:(dj + 1) * 512],
                                   start=(dj == 0), stop=(dj == 7))
                self.inc(ins, "t")
            self.T(ft3)
            twf = self.cnt["t"]

            def fv3(v, fi=fi, P=P, twf=twf):
                self.wait(v, "t", twf)
                for hh in range(2):
                    cs = hh * 256
                    if fi < 16:
                        a = fi
                        p = a % 2
                        c0 = 2048 * p + (a // 2)
                        dst = self.qk_lo[hh][:, c0:c0 + 8 * 255 + 1:8]
                        ins = v.tensor_scalar(dst, P[:, cs:cs + 256],
                                              self.biasp[:, fi:fi + 1], self.qsc[:],
                                              ALU.add, ALU.mult)
                    else:
                        j = fi - 16
                        dst = self.v2[hh][:, j:j + 8 * 255 + 1:8]
                        ins = v.tensor_scalar(dst, P[:, cs:cs + 256],
                                              self.biasp[:, fi:fi + 1], None, ALU.add)
                self.inc(ins, "v")
            self.V(fv3)
            self.g_qkv[pb] = ("v", self.cnt["v"])

        gather_done = self.cnt["v"]
        if hp == 0:
            self.dbg("qk0", self.qk_lo[0][:], (128, SUB), BF16)
            self.dbg("v20", self.v2[0][:], (128, 2048), F32)
            self.dbg("ln1x0", self.ln1x[:], (128, 8 * 512), BF16)

        for hh in range(2):
            self._head_setup(hh, gather_done, skip_negm=(hh == 1))
            if hh == 0:
                self._negm_pre(1, gather_done)
            self._head_attn(hh)
            self._head_ln2(hp, hh)

    def _negm_pre(self, hh, gather_done):
        qk_lo = self.qk_lo[hh]

        def fg(g):
            self.wait(g, "v", gather_done)
            g.reduce_max(self.red[0:64, 3:4], qk_lo[0:64, :], axis=AX.X, apply_absolute_value=True)
            self.inc(g.reduce_max(self.red[0:64, 4:5], qk_lo[64:128, :], axis=AX.X,
                                  apply_absolute_value=True), "v")
            g.wait_ge(self.sem["v"], self.cnt["v"])
            self.inc(g.tensor_mul(self.red[0:64, 5:6], self.red[0:64, 3:4],
                                  self.red[0:64, 4:5]), "v")
        self.V(fg)
        gw = self.cnt["v"]

        def mid():
            PQ = self.P_qkv[1]
            gq_n, gq_v = self.g_qkv[1]

            def ftm(t, gq_n=gq_n, gq_v=gq_v):
                self.wait(t, "v", gw)
                self.wait(t, gq_n, gq_v)
                self.inc(t.matmul(PQ[0:1, 0:1], self.ones_col[0:64, :],
                                  self.red[0:64, 5:6], start=True, stop=True), "t")
            self.T(ftm)
            twm = self.cnt["t"]

            def fvm(v, twm=twm):
                self.wait(v, "t", twm)
                self.inc(v.tensor_copy(self.red[0:1, 7:8], PQ[0:1, 0:1]), "v")
            self.V(fvm)
            vwm = self.cnt["v"]

            def ft(t, vwm=vwm):
                self.wait(t, "v", vwm)
                self.inc(t.matmul(PQ[:, 0:1], self.negsc[:], self.red[0:1, 7:8],
                                  start=True, stop=True), "t")
            self.T(ft)
            tw = self.cnt["t"]
            gnm_n, gnm_v = self.g_negM

            def fv(v, tw=tw, gnm_n=gnm_n, gnm_v=gnm_v, hh=hh):
                self.wait(v, "t", tw)
                self.wait(v, gnm_n, gnm_v)
                self.inc(v.tensor_copy(self.negM[:, hh:hh + 1], PQ[:, 0:1]), "v")
            self.V(fv)
            self.g_qkv[1] = ("v", self.cnt["v"])
            self.negm_pre_ready = self.cnt["v"]
        self.mid_emit.append(mid)

    def _head_setup(self, hh, gather_done, skip_negm=False):
        qk_lo, qk_hi, v2, vaug = self.qk_lo[hh], self.qk_hi[hh], self.v2[hh], self.vaug[hh]

        du = f"du{hh}"
        gh_n, gh_v = self.g_qkhi

        def fy(sy):
            self.wait(sy, "v", gather_done)
            self.wait(sy, gh_n, gh_v)
            self.inc(sy.dma_start(out=qk_hi[64:128, :], in_=qk_lo[0:64, :]), du, 16)
            self.inc(sy.dma_start(out=qk_hi[0:64, :], in_=qk_lo[64:128, :]), du, 16)
        self.Y(fy)
        self.dup_ready = (du, self.cnt[du])

        if skip_negm:
            tprev = self.g_qkhi

            def fvz(v, tprev=tprev):
                self.wait(v, tprev[0], tprev[1])
                v.memset(vaug[:], 0.0)
                self.inc(v.memset(vaug[:, 64:32 * 128:128], 1.0), "v")
            self.V(fvz)
            self.negM_ready = self.negm_pre_ready
        else:
            def fg(g):
                self.wait(g, "v", gather_done)
                g.reduce_max(self.red[0:64, 0:1], qk_lo[0:64, :], axis=AX.X, apply_absolute_value=True)
                self.inc(g.reduce_max(self.red[0:64, 1:2], qk_lo[64:128, :], axis=AX.X,
                                      apply_absolute_value=True), "v")
                g.wait_ge(self.sem["v"], self.cnt["v"])
                self.inc(g.tensor_mul(self.red[0:64, 2:3], self.red[0:64, 0:1],
                                      self.red[0:64, 1:2]), "v")
            self.V(fg)
            gw = self.cnt["v"]
            gm_n, gm_v = self.g_misc

            def ftm(t):
                self.wait(t, "v", gw)
                self.wait(t, gm_n, gm_v)
                self.inc(t.matmul(self.P_misc[0:1, 0:1], self.ones_col[0:64, :],
                                  self.red[0:64, 2:3], start=True, stop=True), "t")
            self.T(ftm)
            twm = self.cnt["t"]

            def fvm(v):
                self.wait(v, "t", twm)
                self.inc(v.tensor_copy(self.red[0:1, 6:7], self.P_misc[0:1, 0:1]), "v")
            self.V(fvm)
            vwm = self.cnt["v"]

            def ft(t):
                self.wait(t, "v", vwm)
                self.inc(t.matmul(self.P_misc[:, 0:1], self.negsc[:], self.red[0:1, 6:7],
                                  start=True, stop=True), "t")
            self.T(ft)
            tw = self.cnt["t"]
            gnm_n, gnm_v = self.g_negM

            def fv(v, hh=hh):
                self.wait(v, "t", tw)
                self.wait(v, gnm_n, gnm_v)
                v.memset(vaug[:], 0.0)
                v.memset(vaug[:, 64:32 * 128:128], 1.0)
                self.inc(v.tensor_copy(self.negM[:, hh:hh + 1], self.P_misc[:, 0:1]), "v")
            self.V(fv)
            self.g_misc = ("v", self.cnt["v"])
            self.negM_ready = self.cnt["v"]

        for grp in range(4):
            gO_n, gO_v = self.g_O

            def ft2(t, grp=grp, gO_n=gO_n, gO_v=gO_v):
                self.wait(t, "v", gather_done)
                self.wait(t, gO_n, gO_v)
                for ib in range(8):
                    ci = grp * 8 + ib
                    par, ki = ci // 16, ci % 16
                    if par == 0:
                        ins = t.transpose(self.P_O[:, 64 * ib:64 * ib + 64],
                                          v2[0:64, ki * 128:(ki + 1) * 128],
                                          self.ident[0:64, 0:64])
                    else:
                        ins = t.transpose(self.P_O[:, 64 * ib:64 * ib + 64],
                                          v2[64:128, ki * 128:(ki + 1) * 128],
                                          self.ident[64:128, 64:128])
                self.inc(ins, "t")
            self.T(ft2)
            twk = self.cnt["t"]

            def fv2(v, grp=grp, twk=twk):
                self.wait(v, "t", twk)
                dst = vaug[:].rearrange("p (cc w) -> p cc w", w=128)[:, grp * 8:grp * 8 + 8, 0:64]
                srcv = self.P_O[:].rearrange("p (cc w) -> p cc w", w=64)
                self.inc(v.tensor_copy(dst, srcv), "v")
            self.V(fv2)
            self.g_O = ("v", self.cnt["v"])
        self.vaug_ready = self.cnt["v"]

    def _head_attn(self, hh):
        qk_lo, qk_hi, vaug = self.qk_lo[hh], self.qk_hi[hh], self.vaug[hh]
        negM_ready = self.negM_ready
        dup_ready = self.dup_ready
        vaug_ready = self.vaug_ready
        nM = self.negM[:, hh:hh + 1]

        pend_epi = [None]
        qtpar = [0]

        def emit_epi(pq, ut):
            two = self.cnt["t"]
            par = qtpar[0] % 2
            ob = self.oraw[0:64, 512 * par:512 * par + 512]
            zrow = 64 + 32 * par
            zb_ = self.oraw[zrow:zrow + 1, 0:1024].bitcast(F32)
            gor_n, gor_v = self.g_oraw[par]
            qtpar[0] += 1

            def fve(v, two=two, ob=ob, zb_=zb_, gor_n=gor_n, gor_v=gor_v):
                self.wait(v, "t", two)
                self.wait(v, gor_n, gor_v)
                v.tensor_copy(ob, self.P_O[0:64, 0:512])
                self.inc(v.tensor_copy(zb_, self.P_O[64:65, 0:512]), "v")
            self.V(fve)
            self.g_O = ("v", self.cnt["v"])

            def fvr(v, zb_=zb_):
                self.inc(v.reciprocal(self.zirow[0:1, 0:512], zb_), "v")
            self.V(fvr)
            vwz = self.cnt["v"]

            def fin(pq=pq, ut=ut, ob=ob, zb_=zb_, par=par, vwz=vwz):
                gm_n, gm_v = self.g_misc

                def ft3(t, vwz=vwz, gm_n=gm_n, gm_v=gm_v):
                    self.wait(t, "v", vwz)
                    self.wait(t, gm_n, gm_v)
                    self.inc(t.matmul(self.P_misc[0:64, 0:512], self.ones_row[0:1, 0:64],
                                      self.zirow[0:1, 0:512], start=True, stop=True), "t")
                self.T(ft3)
                twz = self.cnt["t"]

                def fv3(v, twz=twz, pq=pq, ut=ut, ob=ob):
                    self.wait(v, "t", twz)
                    v.tensor_mul(ob, ob, self.P_misc[0:64, 0:512])
                    for ia in range(8):
                        a2 = 2 * ia + pq
                        r0 = (a2 % 2) * 64
                        dst = self.attn_bf[r0:r0 + 64,
                                           (a2 // 2) * 256 + 64 * ut:(a2 // 2) * 256 + 64 * (ut + 1)]
                        ins = v.tensor_copy(dst, ob[0:64, ia:ia + 8 * 63 + 1:8])
                    self.inc(ins, "v")
                self.V(fv3)
                self.g_misc = ("v", self.cnt["v"])
                self.g_oraw[par] = ("v", self.cnt["v"])
            pend_epi[0] = fin

        pend = [None]
        pend_tile = [None]

        def emit_av(uk, sws, mask_done, nuk):
            gO_n, gO_v = self.g_O

            def ft2(t, uk=uk, sws=sws, mask_done=mask_done, nuk=nuk,
                    gO_n=gO_n, gO_v=gO_v):
                ET = self.etile[uk % 2]
                if mask_done is not None:
                    self.wait(t, "v", mask_done)
                else:
                    self.wait(t, "s", sws)
                if uk == 0:
                    self.wait(t, gO_n, gO_v)
                    self.wait(t, "v", vaug_ready)
                t.matmul(self.P_O[0:128, 0:512], vaug[:, uk * 128:(uk + 1) * 128],
                         ET[:, 0:512], start=(uk == 0), stop=False,
                         skip_group_check=True)
                ins = t.matmul(self.P_O[0:128, 0:512],
                               vaug[:, (16 + uk) * 128:(17 + uk) * 128],
                               ET[:, 512:1024], start=False, stop=(uk == nuk - 1),
                               skip_group_check=True)
                self.inc(ins, "t")
            self.T(ft2)
            self.g_etile[uk % 2] = ("t", self.cnt["t"])

        def flush_pend():
            emit_av(*pend[0])
            pend[0] = None
            if pend_tile[0] is not None:
                ppq, put = pend_tile[0]
                pend_tile[0] = None
                if pend_epi[0] is not None:
                    pend_epi[0]()
                    pend_epi[0] = None
                emit_epi(ppq, put)

        for qt in range(8):
            pq, ut = qt % 2, qt // 2
            q0 = 2048 * pq + 512 * ut
            nuk = 4 * (ut + 1)
            if qt == 2 and self.mid_emit:
                for f in self.mid_emit:
                    f()
                self.mid_emit = []

            for uk in range(nuk):
                sp = uk % 2
                PS = self.P_S[sp]
                ET = self.etile[sp]
                gS_n, gS_v = self.g_S[sp]

                def ft(t, uk=uk, PS=PS, q0=q0, gS_n=gS_n, gS_v=gS_v, first=(uk == 0 and qt == 0)):
                    if first:
                        self.wait(t, dup_ready[0], dup_ready[1])
                        self.wait(t, "v", negM_ready)
                    self.wait(t, gS_n, gS_v)
                    t.matmul(PS[:, 0:512], qk_hi[0:64, 128 * uk:128 * (uk + 1)],
                             qk_lo[0:64, q0:q0 + 512], start=True, stop=True,
                             tile_position=(0, 0), skip_group_check=True)
                    ins = t.matmul(PS[:, 512:1024],
                                   qk_lo[64:128, 2048 + 128 * uk:2048 + 128 * (uk + 1)],
                                   qk_hi[64:128, q0:q0 + 512], start=True, stop=True,
                                   tile_position=(64, 0), skip_group_check=True)
                    self.inc(ins, "t")
                self.T(ft)
                tws = self.cnt["t"]
                gE_n, gE_v = self.g_etile[sp]

                def fs(s, PS=PS, ET=ET, tws=tws, gE_n=gE_n, gE_v=gE_v):
                    self.wait(s, "t", tws)
                    self.wait(s, gE_n, gE_v)
                    self.inc(s.activation(ET[:], PS[:], AF.Exp, bias=nM), "s")
                self.S(fs)
                self.g_S[sp] = ("s", self.cnt["s"])
                sws = self.cnt["s"]

                off = uk * 128 - 512 * ut
                need = []
                for pk in range(2):
                    strict = (pk == 1 and pq == 0)
                    o = off + (1 if strict else 0)
                    if -127 <= o <= 511:
                        need.append((pk, o))
                mask_done = None
                if need:
                    def fv(v, ET=ET, need=tuple(need), sws=sws):
                        self.wait(v, "s", sws)
                        for pk, o in need:
                            start_ = MC - o
                            ins = v.tensor_mul(ET[:, pk * 512:(pk + 1) * 512],
                                               ET[:, pk * 512:(pk + 1) * 512],
                                               self.mmask[:, start_:start_ + 512])
                        self.inc(ins, "v")
                    self.V(fv)
                    mask_done = self.cnt["v"]

                if pend[0] is not None:
                    flush_pend()
                pend[0] = (uk, sws, mask_done, nuk)
            pend_tile[0] = (pq, ut)
        flush_pend()
        if pend_epi[0] is not None:
            pend_epi[0]()
            pend_epi[0] = None
        self.g_negM = ("s", self.cnt["s"])
        self.g_qkhi = ("t", self.cnt["t"])

    def _head_ln2(self, hp, hh):
        h = 2 * hp + hh
        hc = h * 256
        if h == 0:
            self.dbg("attn0", self.attn_bf[:], (128, 8 * 256), BF16)
        vw = self.cnt["v"]

        def fv(v):
            for j in range(8):
                ins = v.tensor_mul(self.x2bf[:, j * 256:(j + 1) * 256],
                                   self.attn_bf[:, j * 256:(j + 1) * 256],
                                   self.attn_bf[:, j * 256:(j + 1) * 256])
            self.inc(ins, "v")
        self.V(fv)
        vw2 = self.cnt["v"]
        gO_n, gO_v = self.g_O

        def ft(t):
            self.wait(t, "v", vw2)
            self.wait(t, gO_n, gO_v)
            for j in range(8):
                t.matmul(self.P_O[:, 0:256], self.ones_bf[:], self.attn_bf[:, j * 256:(j + 1) * 256],
                         start=(j == 0), stop=(j == 7))
            for j in range(8):
                ins = t.matmul(self.P_O[:, 256:512], self.ones_bf[:], self.x2bf[:, j * 256:(j + 1) * 256],
                               start=(j == 0), stop=(j == 7))
            self.inc(ins, "t")
        self.T(ft)
        tw = self.cnt["t"]

        def fsx(v):
            self.wait(v, "t", tw)
            v.tensor_scalar(self.mrow[0:1, 0:256], self.P_O[0:1, 0:256], 1.0 / D, None, ALU.mult)
            self.inc(v.tensor_scalar(self.vrow[0:1, 0:256], self.P_O[0:1, 256:512], 1.0 / D, None, ALU.mult), "v")
        self.V(fsx)
        sw = self._ln_rows(256)
        gm_n, gm_v = self.g_misc

        def ft2(t):
            self.wait(t, "s", sw)
            self.wait(t, gm_n, gm_v)
            self.inc(t.matmul(self.P_misc[:, 0:256], self.ones_row[:], self.rrow[0:1, 0:256],
                              start=True, stop=True), "t")
        self.T(ft2)
        tw2 = self.cnt["t"]

        def fv2(v):
            self.wait(v, "t", tw2)
            for j in range(8):
                xs = self.X[:, j * TOK + hc: j * TOK + hc + 256]
                v.scalar_tensor_tensor(self.lnt[:, 0:256], self.P_O[:, 0:256], -1.0 / D,
                                       self.attn_bf[:, j * 256:(j + 1) * 256], ALU.mult, ALU.add)
                v.tensor_mul(self.lnt[:, 0:256], self.lnt[:, 0:256], self.P_misc[:, 0:256])
                ins = v.tensor_add(xs, xs, self.lnt[:, 0:256])
            self.inc(ins, "v")
        self.V(fv2)
        self.g_misc = ("v", self.cnt["v"])
        self.g_O = ("v", self.cnt["v"])
        vwsp = self.cnt["v"]

        def fgs(g, vwsp=vwsp, hc=hc):
            self.wait(g, "v", vwsp)
            self.inc(g.dma_start(
                out=self.xmid_d.ap().rearrange("j p t -> p j t")[:, :, hc:hc + 256],
                in_=self.X[:].rearrange("p (j t) -> p j t", j=8)[:, :, hc:hc + 256]),
                "xsp", 16)
        self.G(fgs)

    def _ffn_alloc(self):
        self.xc = [self.sb(f"xc{i}", [128, 8 * 512], F32) for i in range(2)]
        self.g_xc = [("v", 0), ("v", 0)]
        self.sK = self.sb("sK", [128, 8], F32)
        self.sKb = self.sb("sKb", [128, 8], BF16)
        self.sKr = self.sb("sKr", [128, 8], BF16)
        self.sKi = self.sb("sKi", [128, 8], BF16)
        self.carry_f = self.sb("carry_f", [128, 4], F32)
        self.carry_i = self.sb("carry_i", [128, 4], F32)
        self.s_ag = self.sb("s_ag", [2, 1024], BF16)
        self.qc = self.sb("qc", [128, 8 * 512], BF16)
        self.kc = self.sb("kc", [128, 8 * 512], BF16)
        self.iqc = self.sb("iqc", [128, 8 * 512], BF16)
        self.ikc = self.sb("ikc", [128, 8 * 512], BF16)
        self.hb = self.sb("hb", [128, 8 * 512], BF16)
        self.ofwd = self.sb("ofwd", [128, 8 * 512], BF16)
        self.xbf2 = self.qc
        self.x2bf2 = self.kc
        self.ln3c = self.sb("ln3c", [128, 8 * 512], BF16)
        self.cmb = self.hb
        self.fout = self.iqc
        self.eexp_bf = [self.sb("eexpb_f", [8, 512], BF16), self.sb("eexpb_i", [8, 512], BF16)]
        self.lrow = self.sb("lrow", [8, 512], F32)
        self.etm = self.sb("etm", [128, 32], F32)
        self.r1 = [self.sb(f"r1_{i}", [128, 1], F32) for i in range(4)]
        self.prod2 = self.sb("prod2", [128, 4 * 512], BF16)
        self.cums = self.sb("cums", [128, 512], F32)
        self.scr = self.sb("scr", [128, 512], F32)
        self.scr2 = self.sb("scr2", [128, 1], F32)
        self.znb = self.sb("znb", [128, 512], F32)
        self.sel = self.sb("sel", [8, 512], BF16)
        self.ninv_r = [self.sb(f"ninv_r{i}", [1, 512], F32) for i in range(2)]
        self.mskb = self.sb("mskb", [2, 1], BF16)
        self.b2pe = self.sb("b2pe", [8, 1], F32)
        self.tmp8 = self.sb("tmp8", [8, 1], F32)
        self.rw2 = self.sb("rw2", [128, 64], BF16)
        self.bf_b = self.sb("bf_b", [8, 1024], BF16)
        self.bi_b = self.sb("bi_b", [8, 1024], BF16)
        self.rb1 = self.sb("rb1", [128, 8], F32)
        self.wbig = [self.sb(f"wbig{i}", [128, 8 * 1024], BF16) for i in range(3)]
        self.wsm = [self.sb(f"wsm{i}", [128, 8 * 512], BF16) for i in range(4)]
        self.Q = [self.ps(f"Q{i}", [128, 512], F32) for i in range(8)]
        self.gQ = [("v", 0)] * 8
        self.bo_extra = []
        self._arena_init()
        bar = self.ffn_bar

        def fy(sy):
            self.wait(sy, bar[0], bar[1])
            self.inc(sy.dma_start(out=self.sel[:], in_=self.dr["sel"]), "lf", 16)
            self.inc(sy.dma_start(out=self.b2pe[:], in_=self.dr["b2"]), "lf", 16)
            self.inc(sy.dma_start(out=self.tmp8[:], in_=self.dr["eb"]), "lf", 16)
            self.inc(sy.dma_start(out=self.rb1[:], in_=self.dr["rb1"]), "lf", 16)
        self.Y(fy)

        def fg(g):
            self.wait(g, bar[0], bar[1])
            self.wait(g, "xsp", 16)
            self.inc(g.dma_start(out=self.mskb[:], in_=self.dr["mskb"]), "lgc", 16)
            self.inc(g.dma_start(out=self.rw2[:].rearrange("p (j e) -> p j e", j=8),
                                 in_=self.dr["rw2_t"].rearrange("j p e -> p j e")), "lgc", 16)
            self.inc(g.dma_start(out=self.bf_b[:], in_=self.dr["b_fwd"]), "lgc", 16)
            self.inc(g.dma_start(out=self.bi_b[:], in_=self.dr["b_inv"]), "lgc", 16)
        self.G(fg)
        self.lf_total = self.cnt["lf"]
        self.lgc0 = self.cnt["lgc"]
        lfw = self.lf_total

        def fv(v):
            self.wait(v, "lf", lfw)
            self.inc(v.tensor_add(self.b2pe[:], self.b2pe[:], self.tmp8[:]), "v")
        self.V(fv)
        self.ffn_misc_v = self.cnt["v"]

    def _xc_load(self, c, slot):
        gn, gv = self.g_xc[slot]
        semn = f"xc{slot}"

        def fy(sy):
            self.wait(sy, "xsp", min(32 * (c + 1), 128))
            self.wait(sy, self.ffn_bar[0], self.ffn_bar[1])
            self.wait(sy, gn, gv)
            col = c * CH
            sy.dma_start(
                out=self.xc[slot][:].rearrange("p (j t) -> p j t", j=8),
                in_=self.xmid_d.ap().rearrange("j p t -> p j t")[:, :, col:col + 512]).then_inc(self.sem[semn], 16)
            self.cnt[semn] += 16
            self.inc(sy.dma_start(out=self.ninv_r[slot][:],
                                  in_=self.dr["ninv"][0:1, col:col + 512]), semn, 16)
        self.Y(fy)
        return (semn, self.cnt[semn])

    def _ln_sq(self, xcs, xc_ready, fv_extra=()):
        def fv(v):
            self.wait(v, xc_ready[0], xc_ready[1])
            for nm, val in fv_extra:
                self.wait(v, nm, val)
            for j in range(8):
                xs = xcs[:, j * 512:(j + 1) * 512]
                v.tensor_copy(self.xbf2[:, j * 512:(j + 1) * 512], xs)
            for j in range(8):
                ins = v.tensor_mul(self.x2bf2[:, j * 512:(j + 1) * 512],
                                   self.xbf2[:, j * 512:(j + 1) * 512],
                                   self.xbf2[:, j * 512:(j + 1) * 512])
            self.inc(ins, "v")
        self.V(fv)
        return self.cnt["v"]

    def _ln_stats(self, vw, banks):
        psA, psB = self.Q[banks[0]], self.Q[banks[1]]
        (gn0, gv0), (gn1, gv1) = self.gQ[banks[0]], self.gQ[banks[1]]

        def ft(t):
            self.wait(t, "v", vw)
            self.wait(t, gn0, gv0)
            self.wait(t, gn1, gv1)
            for j in range(8):
                t.matmul(psA[:], self.ones_bf[:], self.xbf2[:, j * 512:(j + 1) * 512],
                         start=(j == 0), stop=(j == 7))
            for j in range(8):
                ins = t.matmul(psB[:], self.ones_bf[:], self.x2bf2[:, j * 512:(j + 1) * 512],
                               start=(j == 0), stop=(j == 7))
            self.inc(ins, "t")
        self.T(ft)
        tw = self.cnt["t"]

        def fsx(v):
            self.wait(v, "t", tw)
            v.tensor_scalar(self.mrow[0:1, 0:512], psA[0:1, :], 1.0 / D, None, ALU.mult)
            self.inc(v.tensor_scalar(self.vrow[0:1, 0:512], psB[0:1, :], 1.0 / D, None, ALU.mult), "v")
        self.V(fsx)
        sw = self._ln_rows(512)
        self.last_ln_sw = sw
        return sw, tw

    def _ln_apply(self, xcs, dst_bf, sw, banks, accum=False, fv3_extra=()):
        psA, psB = self.Q[banks[0]], self.Q[banks[1]]

        def ft2(t):
            self.wait(t, "s", sw)
            self.inc(t.matmul(psB[:], self.ones_row[:], self.rrow[0:1, 0:512],
                              start=True, stop=True), "t")
        self.T(ft2)
        tw2 = self.cnt["t"]

        base = self.cnt["v"]

        def fv3(v):
            self.wait(v, "t", tw2)
            for nm, val in fv3_extra:
                self.wait(v, nm, val)
            for j in range(8):
                xs = xcs[:, j * 512:(j + 1) * 512]
                v.scalar_tensor_tensor(self.scr[:], psA[:], -1.0 / D, xs, ALU.mult, ALU.add)
                if accum:
                    self.inc(v.scalar_tensor_tensor(dst_bf[:, j * 512:(j + 1) * 512],
                                                    self.scr[:], 1.0, psB[:], ALU.mult, ALU.mult,
                                                    accum_out=self.scr2[:, 0:1]), "v")
                    v.wait_ge(self.sem["v"], self.cnt["v"])
                    ins = v.tensor_add(self.sK[:, j:j + 1], self.sK[:, j:j + 1],
                                       self.scr2[:, 0:1])
                else:
                    self.inc(v.scalar_tensor_tensor(dst_bf[:, j * 512:(j + 1) * 512],
                                                    self.scr[:], 1.0, psB[:], ALU.mult, ALU.mult),
                             "v")
            if accum:
                self.inc(ins, "v")
        self.V(fv3)
        if accum:
            self.ln_dj = [self.cnt["v"]] * 8
        else:
            self.ln_dj = [base + j + 1 for j in range(8)]
        self.gQ[banks[0]] = ("v", self.cnt["v"])
        self.gQ[banks[1]] = ("v", self.cnt["v"])
        return self.cnt["v"]

    def _ln_chunk(self, xcs, xc_ready, dst_bf, accum=False, banks=(0, 1),
                  fv_extra=(), fv3_extra=()):
        vw = self._ln_sq(xcs, xc_ready, fv_extra)
        sw, tw = self._ln_stats(vw, banks)
        av = self._ln_apply(xcs, dst_bf, sw, banks, accum, fv3_extra)
        return av, tw

    def _ffn_ln3_all(self):
        def fvz(v):
            self.inc(v.memset(self.sK[:], 0.0), "v")
        self.V(fvz)

        class ThrowDst:
            def __init__(s2, buf):
                s2.b = buf

            def __getitem__(s2, key):
                return s2.b[:]
        tdst = ThrowDst(self.znb)
        for c in range(NCH):
            slot = c % 2
            rdy = self._xc_load(c, slot)
            self._ln_chunk(self.xc[slot][:], rdy, tdst, accum=True)
            self.g_xc[slot] = ("v", self.cnt["v"])
        self.sK_done = self.cnt["v"]

    def _arena_init(self):
        bar = ("v", self.cnt["v"])
        self.ffn_bar = bar
        self.ar = {
            "big": dict(bufs=self.wbig, n=len(self.wbig), i=0,
                        freed=[True] * len(self.wbig),
                        fcnt=[bar] * len(self.wbig),
                        slot={}, lg={}, pending=[], live=set()),
            "small": dict(bufs=self.wsm, n=len(self.wsm), i=0,
                          freed=[True] * len(self.wsm),
                          fcnt=[bar] * len(self.wsm),
                          slot={}, lg={}, pending=[], live=set()),
        }

    def _ar_emit(self, cls, name):
        a = self.ar[cls]
        slot = a["i"] % a["n"]
        assert a["freed"][slot], (cls, name, slot)
        a["i"] += 1
        fn, fv = a["fcnt"][slot]
        buf = a["bufs"][slot]
        semn = ("lb" if cls == "big" else "ls") + str(slot)

        def fg(g):
            self.wait(g, "xsp", 16)
            self.wait(g, self.ffn_bar[0], self.ffn_bar[1])
            self.wait(g, fn, fv)
            nj = self.dr[name].shape[0]
            self.inc(g.dma_start(out=buf[:].rearrange("p (j d) -> p j d", j=nj),
                                 in_=self.dr[name].rearrange("j p d -> p j d")), semn, 16)
        self.G(fg)
        a["slot"][name] = slot
        a["lg"][name] = (semn, self.cnt[semn])
        a["freed"][slot] = False
        a["live"].add(name)

    def _ar_try(self, cls):
        a = self.ar[cls]
        while a["pending"] and a["freed"][a["i"] % a["n"]]:
            self._ar_emit(cls, a["pending"].pop(0))

    def _ar_req(self, cls, names):
        self.ar[cls]["pending"] += names
        self._ar_try(cls)

    def _ar_free(self, cls, name):
        a = self.ar[cls]
        slot = a["slot"][name]
        assert name in a["live"], (cls, name)
        a["live"].discard(name)
        a["freed"][slot] = True
        a["fcnt"][slot] = ("t", self.cnt["t"])
        self._ar_try(cls)

    def _ar_ap(self, cls, name):
        a = self.ar[cls]
        assert name in a["live"], (cls, name, "not loaded at emission point")
        return a["bufs"][a["slot"][name]], a["lg"][name]

    def _proj_dd(self, wbuf, src8, dst_ps, waits):
        def ft(t):
            for nm, val in waits:
                self.wait(t, nm, val)
            for di in range(8):
                for dj in range(8):
                    ins = t.matmul(dst_ps[:, di:di + 1],
                                   wbuf[:, dj * 1024 + di * 128: dj * 1024 + (di + 1) * 128],
                                   src8[:, dj:dj + 1], start=(dj == 0), stop=(dj == 7))
            self.inc(ins, "t")
        self.T(ft)
        return self.cnt["t"]

    def _ffn_exchange(self):
        self._ar_emit("big", "wk_t")
        self._ar_emit("big", "winv_t")
        self._ar_emit("small", "wf_r")
        self._ar_emit("small", "vi_r")
        lg_wk = self.ar["big"]["lg"]["wk_t"]
        lg_winv = self.ar["big"]["lg"]["winv_t"]
        lg_wf = self.ar["small"]["lg"]["wf_r"]
        lg_vi = self.ar["small"]["lg"]["vi_r"]
        self._ar_req("big", ["wq_t", "rw1_t", "wo_t"]
                     + ["wk_t", "wq_t", "winv_t", "rw1_t", "wo_t"] * (NCH - 1))
        self._ar_req("small", ["vf_r", "uf_r", "wie_r", "ui_r"]
                      + ["wf_r", "vf_r", "uf_r", "wie_r", "vi_r", "ui_r"] * (NCH - 1))
        sK_done = self.sK_done

        def fvc(v):
            self.wait(v, "v", sK_done)
            self.inc(v.tensor_copy(self.sKb[:], self.sK[:]), "v")
        self.V(fvc)
        vwc = self.cnt["v"]

        gn, gv = self.gQ[7]
        tw = self._proj_dd(self.wbig[0], self.sKb, self.Q[7],
                           [lg_wk, ("v", vwc), (gn, gv)])

        def fv2(v):
            self.wait(v, "t", tw)
            self.inc(v.tensor_copy(self.scr[:, 0:8], self.Q[7][:, 0:8]), "v")
        self.V(fv2)
        self.gQ[7] = ("v", self.cnt["v"])
        vw2 = self.cnt["v"]

        def fg(g):
            self.wait(g, "v", vw2)
            self.inc(g.dma_start(out=self.cc_in.ap().rearrange("o (p j) -> p (o j)", j=8),
                                 in_=self.scr[:, 0:8]), "ccd", 16)
            self.wait(g, "ccd", self.cnt["ccd"])
            groups = [[2 * i, 2 * i + 1] for i in range(max(1, self.n_cores // 2))]
            self.inc(g.collective_compute("AllGather", ALU.bypass, replica_groups=groups,
                                          ins=[self.cc_in.ap().opt()],
                                          outs=[self.cc_out.ap().opt()]), "cc")
            self.wait(g, "cc", self.cnt["cc"])
            self.inc(g.dma_start(out=self.s_ag[:], in_=self.cc_out.ap()), "ccd", 16)
        self.G(fg)
        self.exch_lg2 = self.cnt["ccd"]

    def _exchange_tail(self):
        lg2 = self.exch_lg2
        lg_winv = self.ar["big"]["lg"]["winv_t"]
        lg_wf = self.ar["small"]["lg"]["wf_r"]
        lg_vi = self.ar["small"]["lg"]["vi_r"]
        gn6, gv6 = self.gQ[6]

        def ft2(t):
            self.wait(t, "ccd", lg2)
            self.wait(t, "lgc", self.lgc0)
            self.wait(t, gn6, gv6)
            for dj in range(8):
                ins = t.matmul(self.Q[6][:, dj:dj + 1], self.s_ag[:, dj:dj + 8 * 127 + 1:8],
                               self.mskb[:], start=True, stop=True)
            self.inc(ins, "t")
        self.T(ft2)
        tw2 = self.cnt["t"]

        def fv3(v):
            self.wait(v, "t", tw2)
            self.inc(v.tensor_copy(self.sKr[:], self.Q[6][:, 0:8]), "v")
        self.V(fv3)
        self.gQ[6] = ("v", self.cnt["v"])
        sKr_v = self.cnt["v"]

        gn7, gv7 = self.gQ[7]
        tw3 = self._proj_dd(self.wbig[1], self.sKr, self.Q[7],
                            [lg_winv, ("v", sKr_v), (gn7, gv7)])

        def fv4(v):
            self.wait(v, "t", tw3)
            self.inc(v.tensor_copy(self.sKi[:], self.Q[7][:, 0:8]), "v")
        self.V(fv4)
        self.gQ[7] = ("v", self.cnt["v"])
        sKi_v = self.cnt["v"]

        gn5, gv5 = self.gQ[5]

        def ft4(t):
            self.wait(t, lg_wf[0], lg_wf[1])
            self.wait(t, "v", sKr_v)
            self.wait(t, gn5, gv5)
            for krc in range(4):
                for dj in range(8):
                    ins = t.matmul(self.Q[5][:, krc:krc + 1],
                                   self.wsm[0][:, dj * 512 + krc * 128: dj * 512 + (krc + 1) * 128],
                                   self.sKr[:, dj:dj + 1], start=(dj == 0), stop=(dj == 7))
            self.inc(ins, "t")
        self.T(ft4)
        tw4 = self.cnt["t"]

        def ft5(t):
            self.wait(t, lg_vi[0], lg_vi[1])
            self.wait(t, "v", sKi_v)
            for krc in range(4):
                for dj in range(8):
                    ins = t.matmul(self.Q[5][:, 4 + krc:5 + krc],
                                   self.wsm[1][:, dj * 512 + krc * 128: dj * 512 + (krc + 1) * 128],
                                   self.sKi[:, dj:dj + 1], start=(dj == 0), stop=(dj == 7))
            self.inc(ins, "t")
        self.T(ft5)
        tw5 = self.cnt["t"]

        def fv5(v):
            self.wait(v, "t", tw5)
            v.tensor_copy(self.carry_f[:], self.Q[5][:, 0:4])
            self.inc(v.tensor_copy(self.carry_i[:], self.Q[5][:, 4:8]), "v")
        self.V(fv5)
        self.gQ[5] = ("v", self.cnt["v"])
        self.carry_ready = self.cnt["v"]
        self.dbg("skr", self.sKr[:], (128, 8), BF16)
        self.dbg("cf", self.carry_f[:], (128, 4), F32)
        self.dbg("ci2", self.carry_i[:], (128, 4), F32)

    def _mm1024(self, wname, src, dst, waits, evac="v", act=None, bias=None,
                dst_extra=None, pairs=((0, 1), (2, 3)), dj_waits=None):
        wbuf, lg = self._ar_ap("big", wname)
        lgn, lgv = lg
        for dp in range(4):
            b0, b1 = pairs[dp % 2]
            P0, P1 = self.Q[b0], self.Q[b1]
            (gn0, gv0), (gn1, gv1) = self.gQ[b0], self.gQ[b1]
            di0, di1 = 2 * dp, 2 * dp + 1

            def ft(t, di0=di0, di1=di1, P0=P0, P1=P1, gn0=gn0, gv0=gv0, gn1=gn1, gv1=gv1):
                self.wait(t, lgn, lgv)
                for nm, val in waits:
                    self.wait(t, nm, val)
                self.wait(t, gn0, gv0)
                self.wait(t, gn1, gv1)
                for dj in range(8):
                    if dj_waits is not None:
                        self.wait(t, "v", dj_waits[dj])
                    t.matmul(P0[:],
                             wbuf[:, dj * 1024 + di0 * 128: dj * 1024 + (di0 + 1) * 128],
                             src[:, dj * 512:(dj + 1) * 512],
                             start=(dj == 0), stop=(dj == 7))
                    ins = t.matmul(P1[:],
                                   wbuf[:, dj * 1024 + di1 * 128: dj * 1024 + (di1 + 1) * 128],
                                   src[:, dj * 512:(dj + 1) * 512],
                                   start=(dj == 0), stop=(dj == 7))
                self.inc(ins, "t")
            self.T(ft)
            tw = self.cnt["t"]

            if act is not None:
                def fs(s, di0=di0, di1=di1, P0=P0, P1=P1, tw=tw):
                    self.wait(s, "t", tw)
                    b0b = bias[:, di0:di0 + 1] if bias is not None else 0.0
                    b1b = bias[:, di1:di1 + 1] if bias is not None else 0.0
                    s.activation(dst[:, di0 * 512:(di0 + 1) * 512], P0[:], act, bias=b0b)
                    self.inc(s.activation(dst[:, di1 * 512:(di1 + 1) * 512], P1[:], act, bias=b1b), "s")
                self.S(fs)
                self.gQ[b0] = ("s", self.cnt["s"])
                self.gQ[b1] = ("s", self.cnt["s"])
            else:
                def fv(v, di0=di0, di1=di1, P0=P0, P1=P1, tw=tw):
                    self.wait(v, "t", tw)
                    v.tensor_copy(dst[:, di0 * 512:(di0 + 1) * 512], P0[:])
                    self.inc(v.tensor_copy(dst[:, di1 * 512:(di1 + 1) * 512], P1[:]), "v")
                self.V(fv)
                self.gQ[b0] = ("v", self.cnt["v"])
                self.gQ[b1] = ("v", self.cnt["v"])
        return self.cnt["s"] if act is not None else self.cnt["v"]

    def _router(self, br, qsrc, ready):
        hv = self._mm1024("rw1_t", qsrc, self.hb, [ready], act=AF.Gelu, bias=self.rb1)
        gn2, gv2 = self.gQ[4]

        def ft(t):
            self.wait(t, "s", hv)
            self.wait(t, "lgc", self.lgc0)
            self.wait(t, gn2, gv2)
            for dj in range(8):
                ins = t.matmul(self.Q[4][0:8, :], self.rw2[:, dj * 8:(dj + 1) * 8],
                               self.hb[:, dj * 512:(dj + 1) * 512],
                               start=(dj == 0), stop=(dj == 7))
            self.inc(ins, "t")
        self.T(ft)
        tw = self.cnt["t"]

        def fsv(v):
            self.wait(v, "t", tw)
            self.wait(v, "v", self.ffn_misc_v)
            self.inc(v.tensor_scalar_add(self.lrow[:], self.Q[4][0:8, :], self.b2pe[:]), "v")
        self.V(fsv)
        self.gQ[4] = ("v", self.cnt["v"])
        lw = self.cnt["v"]
        gn7, gv7 = self.gQ[5]

        def ft2(t):
            self.wait(t, "v", lw)
            self.wait(t, gn7, gv7)
            for cc in range(4):
                ins = t.transpose(self.Q[5][:, 8 * cc:8 * cc + 8],
                                  self.lrow[0:8, 128 * cc:128 * (cc + 1)],
                                  self.ident[0:8, 0:8])
            self.inc(ins, "t")
        self.T(ft2)
        tw2 = self.cnt["t"]

        def fv3(v):
            self.wait(v, "t", tw2)
            for cc in range(4):
                ins = v.reduce_max(self.r1[cc][:], self.Q[5][:, 8 * cc:8 * cc + 8], axis=AX.X)
            self.inc(ins, "v")
            v.wait_ge(self.sem["v"], self.cnt["v"])
            for cc in range(4):
                ins = v.tensor_scalar(self.r1[cc][:], self.r1[cc][:], -1.0, None, ALU.mult)
            self.inc(ins, "v")
        self.V(fv3)
        vw3 = self.cnt["v"]

        def fs2(s):
            self.wait(s, "v", vw3)
            for cc in range(4):
                ins = s.activation(self.etm[:, 8 * cc:8 * cc + 8],
                                   self.Q[5][:, 8 * cc:8 * cc + 8], AF.Exp,
                                   bias=self.r1[cc][:])
            self.inc(ins, "s")
        self.S(fs2)
        sw2 = self.cnt["s"]

        def fv4(v):
            self.wait(v, "s", sw2)
            for cc in range(4):
                ins = v.reduce_sum(self.r1[cc][:], self.etm[:, 8 * cc:8 * cc + 8], axis=AX.X)
            self.inc(ins, "v")
            v.wait_ge(self.sem["v"], self.cnt["v"])
            for cc in range(4):
                ins = v.reciprocal(self.r1[cc][:], self.r1[cc][:])
            self.inc(ins, "v")
            v.wait_ge(self.sem["v"], self.cnt["v"])
            for cc in range(4):
                ins = v.tensor_scalar(self.etm[:, 8 * cc:8 * cc + 8],
                                      self.etm[:, 8 * cc:8 * cc + 8],
                                      self.r1[cc][:], None, ALU.mult)
            self.inc(ins, "v")
        self.V(fv4)
        self.gQ[5] = ("v", self.cnt["v"])
        vw4 = self.cnt["v"]
        gn2b, gv2b = self.gQ[6]

        def ft3(t):
            self.wait(t, "v", vw4)
            self.wait(t, gn2b, gv2b)
            for cc in range(4):
                ins = t.transpose(self.Q[6][0:8, 128 * cc:128 * (cc + 1)],
                                  self.etm[:, 8 * cc:8 * cc + 8],
                                  self.ident[:, :])
            self.inc(ins, "t")
        self.T(ft3)
        tw3 = self.cnt["t"]

        def fv5(v):
            self.wait(v, "t", tw3)
            self.inc(v.tensor_copy(self.eexp_bf[br][:], self.Q[6][0:8, :]), "v")
        self.V(fv5)
        self.gQ[6] = ("v", self.cnt["v"])
        return self.cnt["v"]

    def _branch_core(self, br, qsrc, ksrc, xv_w, yw_w, q_ready, k_ready, e_ready, carry):
        xbuf, (xln, xlv) = self._ar_ap("small", xv_w)
        ybuf, (yln, ylv) = self._ar_ap("small", yw_w)
        for krc in range(4):
            PX, PY, PE = self.Q[4], self.Q[5], self.Q[6]
            (gn3, gv3), (gn4, gv4), (gn5, gv5) = self.gQ[4], self.gQ[5], self.gQ[6]

            def ft(t, krc=krc, gn3=gn3, gv3=gv3, gn4=gn4, gv4=gv4, gn5=gn5, gv5=gv5):
                self.wait(t, xln, xlv)
                self.wait(t, yln, ylv)
                self.wait(t, "v", max(q_ready, k_ready))
                self.wait(t, gn3, gv3)
                self.wait(t, gn4, gv4)
                for dj in range(8):
                    t.matmul(PX[:], xbuf[:, dj * 512 + krc * 128: dj * 512 + (krc + 1) * 128],
                             qsrc[:, dj * 512:(dj + 1) * 512], start=(dj == 0), stop=(dj == 7))
                for dj in range(8):
                    ins = t.matmul(PY[:], ybuf[:, dj * 512 + krc * 128: dj * 512 + (krc + 1) * 128],
                                   ksrc[:, dj * 512:(dj + 1) * 512], start=(dj == 0), stop=(dj == 7))
                self.inc(ins, "t")
                self.wait(t, "lf", self.lf_total)
                self.wait(t, "v", e_ready)
                self.wait(t, gn5, gv5)
                self.inc(t.matmul(PE[:], self.sel[:, krc * 128:(krc + 1) * 128],
                                  self.eexp_bf[br][:], start=True, stop=True), "t")
            self.T(ft)
            tw = self.cnt["t"]

            def fv(v, krc=krc, tw=tw):
                self.wait(v, "t", tw)
                if krc == 0:
                    self.wait(v, "v", self.carry_ready)
                self.inc(v.tensor_tensor_scan(self.cums[:], self.ones_col[:].broadcast_to([128, 512]),
                                              PY[:], carry[:, krc:krc + 1], ALU.mult, ALU.add), "v")
                v.wait_ge(self.sem["v"], self.cnt["v"])
                v.tensor_copy(carry[:, krc:krc + 1], self.cums[:, 511:512])
                v.tensor_mul(self.scr[:], PX[:], self.cums[:])
                ins = v.tensor_mul(self.prod2[:, krc * 512:(krc + 1) * 512], self.scr[:], PE[:])
                self.inc(ins, "v")
            self.V(fv)
            self.gQ[4] = ("v", self.cnt["v"])
            self.gQ[5] = ("v", self.cnt["v"])
            self.gQ[6] = ("v", self.cnt["v"])
        return self.cnt["v"]

    def _branch_out(self, br, u_w, b_sb, prod_ready, col, out_dst, add_prev=None):
        rdy = self.cur_rdy
        nslot = self.cur_slot
        """out = U.T@prod2 + b.T@eexp_bf; scale by ninv (and alpha for inv);
        write bf16 to out_dst (8 chunks); if add_prev: add ofwd."""
        ubuf, (uln, ulv) = self._ar_ap("small", u_w)
        vwz0 = self.cnt["v"]
        bo_extra = tuple(self.bo_extra)

        def fvz(v, bo_extra=bo_extra):
            self.wait(v, "lf", self.lf_total)
            self.wait(v, rdy[0], rdy[1])
            self.wait(v, "v", prod_ready)
            for nm, val in bo_extra:
                self.wait(v, nm, val)
            if br == 1:
                ins = v.tensor_scalar(self.vrow[0:1, 0:512], self.ninv_r[nslot][0:1, :],
                                      self.alpha[0:1, 0:1], None, ALU.mult)
            else:
                ins = v.tensor_copy(self.vrow[0:1, 0:512], self.ninv_r[nslot][0:1, :])
            self.inc(ins, "v")
        self.V(fvz)
        vwz = self.cnt["v"]
        gn6, gv6 = self.gQ[6]

        def ftz(t):
            self.wait(t, "v", vwz)
            self.wait(t, gn6, gv6)
            self.inc(t.matmul(self.Q[6][:], self.ones_row[:], self.vrow[0:1, 0:512],
                              start=True, stop=True), "t")
        self.T(ftz)
        twz = self.cnt["t"]

        def fvz2(v):
            self.wait(v, "t", twz)
            self.inc(v.tensor_copy(self.znb[:], self.Q[6][:]), "v")
        self.V(fvz2)
        self.gQ[6] = ("v", self.cnt["v"])
        znb_v = self.cnt["v"]

        for di in range(8):
            pb = di % 2
            P = self.Q[pb]
            gn, gv = self.gQ[pb]

            def ft(t, di=di, P=P, gn=gn, gv=gv):
                self.wait(t, uln, ulv)
                self.wait(t, "v", prod_ready)
                self.wait(t, gn, gv)
                for krc in range(4):
                    t.matmul(P[:], ubuf[:, krc * 1024 + di * 128: krc * 1024 + (di + 1) * 128],
                             self.prod2[:, krc * 512:(krc + 1) * 512],
                             start=(krc == 0), stop=False)
                self.wait(t, "lgc", self.lgc0)
                ins = t.matmul(P[:], b_sb[:, di * 128:(di + 1) * 128], self.eexp_bf[br][:],
                               start=False, stop=True)
                self.inc(ins, "t")
            self.T(ft)
            tw = self.cnt["t"]

            def fv(v, di=di, P=P, tw=tw):
                self.wait(v, "t", tw)
                self.wait(v, "v", znb_v)
                if add_prev is None:
                    ins = v.tensor_mul(out_dst[:, di * 512:(di + 1) * 512], P[:], self.znb[:])
                else:
                    v.tensor_mul(self.scr[:], P[:], self.znb[:])
                    ins = v.tensor_add(out_dst[:, di * 512:(di + 1) * 512], self.scr[:],
                                       add_prev[:, di * 512:(di + 1) * 512])
                self.inc(ins, "v")
            self.V(fv)
            self.gQ[pb] = ("v", self.cnt["v"])
        return self.cnt["v"]

    def _ffn_chunk(self, c, ln3v, rdy):
        col = c * CH
        slot = c % 2
        self.cur_slot = slot
        xcs = self.xc[slot][:]
        self.cur_rdy = rdy
        ln3_dj = self.ln_dj

        rdy_next = self._xc_load(c + 1, 1 - slot) if c + 1 < NCH else None

        kv = self._mm1024("wk_t", self.ln3c, self.kc, [], dj_waits=ln3_dj)
        self._ar_free("big", "wk_t")
        qv = self._mm1024("wq_t", self.ln3c, self.qc, [("v", ln3v)])
        self._ar_free("big", "wq_t")

        iqv = self._mm1024("winv_t", self.qc, self.iqc, [("v", qv)])
        ikv = self._mm1024("winv_t", self.kc, self.ikc, [("v", kv)])
        if c == 0:
            self._exchange_tail()
        self._ar_free("big", "winv_t")

        ef = self._router(0, self.qc, ("v", qv))
        ei = self._router(1, self.iqc, ("v", iqv))
        self._ar_free("big", "rw1_t")

        pf = self._branch_core(0, self.qc, self.kc, "vf_r", "wf_r", qv, kv, ef, self.carry_f)
        self._ar_free("small", "vf_r")
        self._ar_free("small", "wf_r")
        of = self._branch_out(0, "uf_r", self.bf_b, pf, col, self.ofwd)
        self._ar_free("small", "uf_r")

        pi = self._branch_core(1, self.iqc, self.ikc, "wie_r", "vi_r", iqv, ikv, ei, self.carry_i)
        self._ar_free("small", "wie_r")
        self._ar_free("small", "vi_r")

        early_ln3 = rdy_next is not None and not (self.debug and c == 0)
        if early_ln3:
            ln3_vw = self._ln_sq(self.xc[1 - slot][:], rdy_next,
                                 fv_extra=(("v", pf),))

        ci = self._branch_out(1, "ui_r", self.bi_b, pi, col, self.cmb,
                              add_prev=self.ofwd)
        self._ar_free("small", "ui_r")

        if early_ln3:
            ln3_sw, ln3_tw = self._ln_stats(ln3_vw, (2, 3))

        fv_ = self._mm1024("wo_t", self.cmb, self.fout, [("v", ci)],
                           pairs=((0, 1), (6, 7)))
        self._ar_free("big", "wo_t")

        if c == 0 and self.debug:
            self.dbg("qc0", self.qc[:], (128, 8 * 512), BF16)
            self.dbg("ef0", self.eexp_bf[0][:], (8, 512), BF16)
            self.dbg("prod20", self.prod2[:], (128, 4 * 512), BF16)
            self.dbg("ofwd0", self.ofwd[:], (128, 8 * 512), BF16)
            self.dbg("fout0", self.fout[:], (128, 8 * 512), BF16)
            dbg_bar = (("st", self.cnt["st"]),)
        else:
            dbg_bar = ()

        if early_ln3:
            ln3v_next = self._ln_apply(self.xc[1 - slot][:], self.ln3c,
                                       ln3_sw, (2, 3))
        elif rdy_next is not None:
            ln3v_next, ln3_tw = self._ln_chunk(
                self.xc[1 - slot][:], rdy_next, self.ln3c, banks=(2, 3),
                fv_extra=(("v", ci),) + dbg_bar,
                fv3_extra=(("v", ci),) + dbg_bar)
        else:
            ln3v_next, ln3_tw = None, None

        def fvq(v, fv_=fv_, ln3_tw=ln3_tw):
            self.wait(v, "v", fv_)
            if ln3_tw is not None:
                self.wait(v, "t", ln3_tw)
            for j in range(8):
                ins = v.tensor_mul(self.x2bf2[:, j * 512:(j + 1) * 512],
                                   self.fout[:, j * 512:(j + 1) * 512],
                                   self.fout[:, j * 512:(j + 1) * 512])
            self.inc(ins, "v")
        self.V(fvq)
        vwq = self.cnt["v"]
        psA, psB = self.Q[4], self.Q[5]
        (gn0, gv0), (gn1, gv1) = self.gQ[4], self.gQ[5]

        def ft(t):
            self.wait(t, "v", vwq)
            self.wait(t, gn0, gv0)
            self.wait(t, gn1, gv1)
            for j in range(8):
                t.matmul(psA[:], self.ones_bf[:], self.fout[:, j * 512:(j + 1) * 512],
                         start=(j == 0), stop=(j == 7))
            for j in range(8):
                ins = t.matmul(psB[:], self.ones_bf[:], self.x2bf2[:, j * 512:(j + 1) * 512],
                               start=(j == 0), stop=(j == 7))
            self.inc(ins, "t")
        self.T(ft)
        tw = self.cnt["t"]

        def fsx(v):
            self.wait(v, "t", tw)
            v.tensor_scalar(self.mrow[0:1, 0:512], psA[0:1, :], 1.0 / D, None, ALU.mult)
            self.inc(v.tensor_scalar(self.vrow[0:1, 0:512], psB[0:1, :], 1.0 / D, None, ALU.mult), "v")
        self.V(fsx)
        sw = self._ln_rows(512)

        def ft2(t):
            self.wait(t, "s", sw)
            self.inc(t.matmul(psB[:], self.ones_row[:], self.rrow[0:1, 0:512],
                              start=True, stop=True), "t")
        self.T(ft2)
        tw2 = self.cnt["t"]

        def fv2(v):
            self.wait(v, "t", tw2)
            for j in range(8):
                xs = xcs[:, j * 512:(j + 1) * 512]
                v.scalar_tensor_tensor(self.scr[:], psA[:], -1.0 / D,
                                       self.fout[:, j * 512:(j + 1) * 512], ALU.mult, ALU.add)
                v.tensor_mul(self.scr[:], self.scr[:], psB[:])
                ins = v.tensor_add(xs, xs, self.scr[:])
            self.inc(ins, "v")
        self.V(fv2)
        self.gQ[4] = ("v", self.cnt["v"])
        self.gQ[5] = ("v", self.cnt["v"])
        vfin = self.cnt["v"]

        def fy(sy):
            self.wait(sy, "v", vfin)
            dst = self.dr["y"].rearrange("j p t -> p j t")[:, :, col:col + 512]
            self.inc(sy.dma_start(out=dst,
                                  in_=xcs.rearrange("p (j t) -> p j t", j=8)), "st", 16)
        self.Y(fy)
        self.g_xc[slot] = ("st", self.cnt["st"])
        return ln3v_next, rdy_next

    def _finish(self):
        stw = self.cnt["st"]

        def fy(sy):
            self.wait(sy, "st", stw)
        self.Y(fy)



def _prep_core_inputs(inputs, core):
    b, half = core // 2, core % 2
    x = np.asarray(inputs["L0_attn_inputs"])
    d = {}
    xt = np.ascontiguousarray(x[b, half * TOK:(half + 1) * TOK, :].T)
    d["x"] = np.ascontiguousarray(xt.reshape(8, 128, TOK))

    ninv = (1.0 / (np.arange(TOK, dtype=np.float64) + 1 + half * TOK)).astype(np.float32)
    d["ninv"] = ninv.reshape(1, TOK)
    mskb = np.zeros((2, 1), np.float32)
    if half == 1:
        mskb[0, 0] = 1.0
    d["mskb"] = mskb
    return d


def _prep_shared_inputs(inputs):
    d = {}
    Wqkv = np.asarray(inputs["L0_attn_qkv_kernel"])
    bias = np.asarray(inputs["L0_attn_qkv_bias"]).reshape(3072)
    perm = []
    for a in range(16):
        perm += list(range(a * 192, a * 192 + 128))
    for a in range(16):
        perm += list(range(a * 192 + 128, a * 192 + 192))
    perm = np.array(perm)
    Wp = np.ascontiguousarray(Wqkv[:, perm])
    bp = np.ascontiguousarray(bias[perm])
    d["wqkv"] = np.ascontiguousarray(Wp.reshape(8, 128, 3072))
    d["biasp"] = np.ascontiguousarray(bp.reshape(24, 128).T)

    def bigT(w):
        return np.ascontiguousarray(np.ascontiguousarray(np.asarray(w).T).reshape(8, 128, 1024))
    d["wq_t"] = bigT(inputs["L0_ffn_W_Q"])
    d["wk_t"] = bigT(inputs["L0_ffn_W_K"])
    d["winv_t"] = bigT(inputs["L0_ffn_W_inv"])
    d["rw1_t"] = bigT(inputs["L0_ffn_router_w1"])
    d["wo_t"] = bigT(inputs["L0_ffn_W_O"])
    rw2 = np.asarray(inputs["L0_ffn_router_w2"])
    d["rw2_t"] = np.ascontiguousarray(np.ascontiguousarray(rw2.T).reshape(8, 128, 8))

    def vr(w):
        a = np.transpose(np.asarray(w), (1, 0, 2)).reshape(1024, 512)
        return np.ascontiguousarray(np.ascontiguousarray(a).reshape(8, 128, 512))

    def ur(w):
        a = np.transpose(np.asarray(w), (0, 2, 1)).reshape(512, 1024)
        return np.ascontiguousarray(np.ascontiguousarray(a).reshape(4, 128, 1024))
    d["vf_r"] = vr(inputs["L0_ffn_V_fwd"])
    d["wf_r"] = vr(inputs["L0_ffn_W_fwd"])
    d["wie_r"] = vr(inputs["L0_ffn_W_inv_exp"])
    d["vi_r"] = vr(inputs["L0_ffn_V_inv"])
    d["uf_r"] = ur(inputs["L0_ffn_U_fwd"])
    d["ui_r"] = ur(inputs["L0_ffn_U_inv"])
    d["b_fwd"] = np.ascontiguousarray(np.asarray(inputs["L0_ffn_b_fwd"], np.float32))
    d["b_inv"] = np.ascontiguousarray(np.asarray(inputs["L0_ffn_b_inv"], np.float32))
    rb1 = np.asarray(inputs["L0_ffn_router_b1"]).reshape(8, 128)
    d["rb1"] = np.ascontiguousarray(rb1.T)
    d["b2"] = np.ascontiguousarray(np.asarray(inputs["L0_ffn_router_b2"], np.float32).reshape(8, 1))
    d["eb"] = np.ascontiguousarray(np.asarray(inputs["L0_ffn_expert_bias"], np.float32).reshape(8, 1))
    d["alpha"] = np.asarray(inputs["L0_ffn_alpha_bi"], np.float32).reshape(1, 1)

    d["ident"] = np.eye(128, dtype=np.float32)
    mm = np.zeros((128, MW), dtype=np.float32)
    for i in range(128):
        mm[i, i + MC:] = 1.0
    d["mmask"] = mm.astype(ml_dtypes.bfloat16)
    sel = np.zeros((8, 512), np.float32)
    for krc in range(4):
        for p in range(128):
            sel[2 * krc + p // 64, krc * 128 + p] = 1.0
    d["sel"] = sel.astype(ml_dtypes.bfloat16)
    return d


def kernel(**inputs):
    n_cores = 8
    shared = _prep_shared_inputs(inputs)
    in_maps = []
    for c in range(n_cores):
        m = dict(shared)
        m.update(_prep_core_inputs(inputs, c))
        in_maps.append(m)
    b = Builder(n_cores=n_cores, debug=False)
    nc = b.build()
    res = run_bass_kernel_spmd(nc, in_maps, core_ids=list(range(n_cores)))
    out = np.zeros((4, 4096, 1024), np.float32)
    for c in range(n_cores):
        y = np.asarray(res.results[c]["y"])
        bb, half = c // 2, c % 2
        out[bb, half * TOK:(half + 1) * TOK, :] = y.reshape(1024, TOK).T
    return out


# version 19
# speedup vs baseline: 1.0022x; 1.0022x over previous
import contextlib
import numpy as np
import ml_dtypes
import concourse.bass as bass
import concourse.mybir as mybir
from concourse.bass_utils import run_bass_kernel_spmd

F32 = mybir.dt.float32
BF16 = mybir.dt.bfloat16
AF = mybir.ActivationFunctionType
ALU = mybir.AluOpType
AX = mybir.AxisListType

D = 1024
TOK = 2048
NH = 8
SUB = 4096
NCH = 4
CH = 512
SCALE = float(1.0 / np.sqrt(2.0 * D))
EPS = 1e-5
ZC = float(2.0 ** 20)
MC = 512
MW = 1152


class _MockInstr:
    def then_inc(self, sem, k=1, **kw):
        return self

    def annotate(self, *a, **kw):
        return self


class _Mock:
    def __getattr__(self, name):
        if name == "wait_ge":
            return lambda *a, **kw: None
        return lambda *a, **kw: _MockInstr()


class Builder:
    def __init__(self, n_cores=8, debug=False):
        self.n_cores = n_cores
        self.debug = debug
        self.nc = bass.Bass("TRN2", target_bir_lowering=False,
                            detect_race_conditions=False)
        self.cnt = {}
        self.sem = {}
        self.dbg_list = []
        self.prog = []
        self.pass2 = False

    def mksem(self, name):
        s = self.es.enter_context(self.nc.semaphore(name))
        self.sem[name] = s
        self.cnt[name] = 0

    def inc(self, instr, name, k=1):
        instr.then_inc(self.sem[name], k)
        self.cnt[name] += k
        return self.cnt[name]

    def wait(self, eng, name, val):
        if not self.pass2:
            assert val <= self.cnt[name], (name, val, self.cnt[name])
        if val > 0:
            eng.wait_ge(self.sem[name], val)

    def sb(self, name, shape, dtype):
        return self.es.enter_context(self.nc.sbuf_tensor("sb_" + name, shape, dtype))

    def ps(self, name, shape, dtype=F32):
        return self.es.enter_context(self.nc.psum_tensor(name, shape, dtype))

    def T(self, f): self._rec("tensor", f)
    def V(self, f): self._rec("vector", f)
    def S(self, f): self._rec("scalar", f)
    def G(self, f): self._rec("gpsimd", f)
    def Y(self, f): self._rec("sync", f)

    def _rec(self, eng, f):
        self.prog.append((eng, f))
        f(_Mock())

    def dbg(self, name, ap, shape, dtype=F32):
        if not self.debug:
            return
        t = self.nc.dram_tensor("dbg_" + name, list(shape), dtype,
                                kind="ExternalOutput").ap()
        self.dbg_list.append(name)
        w = {k: self.cnt[k] for k in ("v", "t", "s", "g")}

        def f(sy):
            for k, vv in w.items():
                self.wait(sy, k, vv)
            self.inc(sy.dma_start(out=t, in_=ap), "st", 16)
        self.Y(f)

    def build(self):
        nc = self.nc
        dr = {}
        dr["x"] = nc.dram_tensor("x", [8, 128, TOK], F32, kind="ExternalInput").ap()
        dr["wqkv"] = nc.dram_tensor("wqkv", [8, 128, 3072], F32, kind="ExternalInput").ap()
        dr["biasp"] = nc.dram_tensor("biasp", [128, 24], F32, kind="ExternalInput").ap()
        for nm in ["wq_t", "wk_t", "winv_t", "rw1_t", "wo_t"]:
            dr[nm] = nc.dram_tensor(nm, [8, 128, 1024], F32, kind="ExternalInput").ap()
        dr["rw2_t"] = nc.dram_tensor("rw2_t", [8, 128, 8], F32, kind="ExternalInput").ap()
        for nm in ["vf_r", "wf_r", "wie_r", "vi_r"]:
            dr[nm] = nc.dram_tensor(nm, [8, 128, 512], F32, kind="ExternalInput").ap()
        for nm in ["uf_r", "ui_r"]:
            dr[nm] = nc.dram_tensor(nm, [4, 128, 1024], F32, kind="ExternalInput").ap()
        dr["b_fwd"] = nc.dram_tensor("b_fwd", [8, 1024], F32, kind="ExternalInput").ap()
        dr["b_inv"] = nc.dram_tensor("b_inv", [8, 1024], F32, kind="ExternalInput").ap()
        dr["rb1"] = nc.dram_tensor("rb1", [128, 8], F32, kind="ExternalInput").ap()
        dr["b2"] = nc.dram_tensor("b2", [8, 1], F32, kind="ExternalInput").ap()
        dr["eb"] = nc.dram_tensor("eb", [8, 1], F32, kind="ExternalInput").ap()
        dr["alpha"] = nc.dram_tensor("alpha", [1, 1], F32, kind="ExternalInput").ap()
        dr["ident"] = nc.dram_tensor("ident", [128, 128], F32, kind="ExternalInput").ap()
        dr["mmask"] = nc.dram_tensor("mmask", [128, MW], BF16, kind="ExternalInput").ap()
        dr["sel"] = nc.dram_tensor("sel", [8, 512], BF16, kind="ExternalInput").ap()
        dr["ninv"] = nc.dram_tensor("ninv", [1, TOK], F32, kind="ExternalInput").ap()
        dr["mskb"] = nc.dram_tensor("mskb", [2, 1], F32, kind="ExternalInput").ap()
        dr["y"] = nc.dram_tensor("y", [8, 128, TOK], F32, kind="ExternalOutput").ap()
        self.cc_in = nc.dram_tensor("cc_in", [1, 1024], F32)
        self.cc_out = nc.dram_tensor("cc_out", [2, 1024], F32)
        self.xmid_d = nc.dram_tensor("xmid_d", [8, 128, TOK], F32)
        self.dr = dr

        with contextlib.ExitStack() as es:
            self.es = es
            for s in (["ld", "lg", "st", "cc", "t", "v", "s", "g", "du0", "du1", "lgc"]
                      + [f"lb{i}" for i in range(3)] + [f"ls{i}" for i in range(4)]
                      + ["xsp", "xc0", "xc1", "lf", "ccd"]):
                self.mksem(s)
            self._persistent()
            self._phase0()
            with contextlib.ExitStack() as attn_es:
                outer = self.es
                self.es = attn_es
                self._attn_alloc()
                self.es = outer
                for hp in range(NH // 2):
                    self._attn_pair(hp)
                self.dbg("xmid", self.X[:], (128, 8 * TOK), F32)
            with contextlib.ExitStack() as ffn_es:
                outer = self.es
                self.es = ffn_es
                self._ffn_alloc()
                self.es = outer
                self._ffn_ln3_all()
                self.dbg("sk", self.sK[:], (128, 8), F32)
                self._ffn_exchange()
                rdy = self._xc_load(0, 0)
                ln3v, _ = self._ln_chunk(self.xc[0][:], rdy, self.ln3c,
                                         banks=(2, 3))
                for c in range(NCH):
                    ln3v, rdy = self._ffn_chunk(c, ln3v, rdy)
            self._finish()
            self.pass2 = True
            self.cnt = {k: 0 for k in self.cnt}
            with nc.Block() as block:
                for eng in ["tensor", "scalar", "vector", "gpsimd", "sync"]:
                    fs = [f for e, f in self.prog if e == eng]
                    if not fs:
                        continue
                    getattr(block, eng)(lambda handle, fs=fs: [f(handle) for f in fs])
        return nc

    def _persistent(self):
        self.ident = self.sb("ident", [128, 128], F32)
        self.ones_bf = self.sb("ones_bf", [128, 128], BF16)
        self.ones_row = self.sb("ones_row", [1, 128], F32)
        self.negsc = self.sb("negsc", [1, 128], F32)
        self.ones_col = self.sb("ones_col", [128, 1], F32)
        self.neg8 = self.sb("neg8", [1, 8], F32)
        self.alpha = self.sb("alpha", [1, 1], F32)
        self.biasp = self.sb("biasp", [128, 24], F32)
        self.qsc = self.sb("qsc", [128, 1], F32)
        self.epsc = self.sb("epsc", [1, 1], F32)
        self.mrow = self.sb("mrow", [1, 512], F32)
        self.vrow = self.sb("vrow", [1, 512], F32)
        self.rrow = self.sb("rrow", [1, 512], F32)
        self.zirow = self.sb("zirow", [1, 512], F32)

    def _phase0(self):
        dr = self.dr

        def fy(sy):
            self.inc(sy.dma_start(out=self.ident[:], in_=dr["ident"]), "ld", 16)
            self.inc(sy.dma_start(out=self.biasp[:], in_=dr["biasp"]), "ld", 16)
            self.inc(sy.dma_start(out=self.alpha[:], in_=dr["alpha"]), "ld", 16)
        self.Y(fy)

        def fv(v):
            v.memset(self.ones_bf[:], 1.0)
            v.memset(self.ones_row[:], 1.0)
            v.memset(self.negsc[:], -1.05)
            v.memset(self.ones_col[:], 1.0)
            v.memset(self.neg8[:], -1.0)
            v.memset(self.qsc[0:64, :], SCALE)
            v.memset(self.epsc[:], EPS)
            self.inc(v.memset(self.qsc[64:128, :], 1.0), "v")
        self.V(fv)
        self.ld_total = self.cnt["ld"]

    def _attn_alloc(self):
        self.X = self.sb("X", [128, 8 * TOK], F32)
        self.mmask = self.sb("mmask", [128, MW], BF16)
        self.wqkv_sb = self.sb("wqkv_sb", [128, 8 * 3072], BF16)
        self.qk_lo = [self.sb(f"qk_lo{i}", [128, SUB], BF16) for i in range(2)]
        qkh = self.sb("qk_hi", [128, SUB], BF16)
        self.qk_hi = [qkh, qkh]
        self.g_qkhi = ("t", 0)
        self.v2 = [self.sb(f"v2{i}", [128, 2048], F32) for i in range(2)]
        va = self.sb("vaug", [128, 32 * 128], BF16)
        self.vaug = [va, va]
        self.ln1x = self.sb("ln1x", [128, 8 * 512], BF16)
        self.xbf = self.sb("xbf", [128, 8 * 512], BF16)
        self.x2bf = self.sb("x2bf", [128, 8 * 512], BF16)
        self.attn_bf = self.sb("attn_bf", [128, 8 * 256], BF16)
        self.etile = [self.sb(f"etile{i}", [128, 1024], BF16) for i in range(2)]
        self.negM = self.sb("negM", [128, 2], F32)
        self.oraw = self.sb("oraw", [128, 1024], BF16)
        self.lnt = self.x2bf[0:128, 0:512].bitcast(F32)
        self.red = self.sb("red", [64, 8], F32)
        self.P_qkv = [self.ps(f"P_qkv{i}", [128, 512], F32) for i in range(2)]
        self.P_S = [self.ps(f"P_S{i}", [128, 1024], F32) for i in range(2)]
        self.P_O = self.ps("P_O", [128, 512], F32)
        self.P_misc = self.ps("P_misc", [128, 512], F32)
        self.g_qkv = [("v", 0), ("v", 0)]
        self.g_S = [("s", 0), ("s", 0)]
        self.g_etile = [("t", 0), ("t", 0)]
        self.g_O = ("v", 0)
        self.g_misc = ("v", 0)
        self.g_negM = ("s", 0)
        self.g_oraw = [("v", 0), ("v", 0)]
        self.mid_emit = []

        def fy0(sy):
            self.inc(sy.dma_start(out=self.X[:].rearrange("p (j t) -> p j t", j=8),
                                  in_=self.dr["x"].rearrange("j p t -> p j t")), "ld", 16)
            self.inc(sy.dma_start(out=self.mmask[:], in_=self.dr["mmask"]), "ld", 16)
        self.Y(fy0)
        self.ld_total = self.cnt["ld"]

        def fg(g):
            self.inc(g.dma_start(out=self.wqkv_sb[:].rearrange("p (j f) -> p j f", j=8),
                                 in_=self.dr["wqkv"].rearrange("j p f -> p j f")), "lg", 16)
        self.G(fg)
        self.lg0_total = self.cnt["lg"]

    def _ln_rows(self, ncols):
        def fv(v):
            v.tensor_mul(self.zirow[0:1, 0:ncols], self.mrow[0:1, 0:ncols], self.mrow[0:1, 0:ncols])
            self.inc(v.tensor_sub(self.vrow[0:1, 0:ncols], self.vrow[0:1, 0:ncols],
                                  self.zirow[0:1, 0:ncols]), "v")
        self.V(fv)
        vw = self.cnt["v"]

        def fs(s):
            self.wait(s, "v", vw)
            s.activation(self.vrow[0:1, 0:ncols], self.vrow[0:1, 0:ncols], AF.Ln, bias=self.epsc[:])
            self.inc(s.activation(self.rrow[0:1, 0:ncols], self.vrow[0:1, 0:ncols],
                                  AF.Exp, scale=-0.5), "s")
        self.S(fs)
        return self.cnt["s"]

    def _attn_pair(self, hp):
        X = self.X
        pc = hp * 512

        ldw = self.cnt["ld"]

        def fv(v):
            self.wait(v, "ld", self.ld_total)
            for j in range(8):
                xs = X[:, j * TOK + pc: j * TOK + pc + 512]
                v.tensor_copy(self.xbf[:, j * 512:(j + 1) * 512], xs)
            for j in range(8):
                ins = v.tensor_mul(self.x2bf[:, j * 512:(j + 1) * 512],
                                   self.xbf[:, j * 512:(j + 1) * 512],
                                   self.xbf[:, j * 512:(j + 1) * 512])
            self.inc(ins, "v")
        self.V(fv)
        vw = self.cnt["v"]

        psA, psB = self.P_qkv
        (sgn0, sgv0), (sgn1, sgv1) = self.g_qkv

        def ft(t, sgn0=sgn0, sgv0=sgv0, sgn1=sgn1, sgv1=sgv1, vw=vw):
            self.wait(t, "v", vw)
            self.wait(t, sgn0, sgv0)
            self.wait(t, sgn1, sgv1)
            for j in range(8):
                t.matmul(psA[:], self.ones_bf[:], self.xbf[:, j * 512:(j + 1) * 512],
                         start=(j == 0), stop=(j == 7))
            for j in range(8):
                ins = t.matmul(psB[:], self.ones_bf[:], self.x2bf[:, j * 512:(j + 1) * 512],
                               start=(j == 0), stop=(j == 7))
            self.inc(ins, "t")
        self.T(ft)
        tw = self.cnt["t"]

        def fsx(v):
            self.wait(v, "t", tw)
            v.tensor_scalar(self.mrow[0:1, 0:512], psA[0:1, 0:512], 1.0 / D, None, ALU.mult)
            self.inc(v.tensor_scalar(self.vrow[0:1, 0:512], psB[0:1, 0:512], 1.0 / D, None, ALU.mult), "v")
        self.V(fsx)
        sw = self._ln_rows(512)

        def ft2(t):
            self.wait(t, "s", sw)
            self.inc(t.matmul(psB[:, 0:512], self.ones_row[:], self.rrow[0:1, 0:512],
                              start=True, stop=True), "t")
        self.T(ft2)
        tw2 = self.cnt["t"]

        def fv2(v):
            self.wait(v, "t", tw2)
            for j in range(8):
                xs = X[:, j * TOK + pc: j * TOK + pc + 512]
                v.scalar_tensor_tensor(self.xbf[:, j * 512:(j + 1) * 512],
                                       psA[:, 0:512], -1.0 / D, xs, ALU.mult, ALU.add)
                ins = v.tensor_mul(self.ln1x[:, j * 512:(j + 1) * 512],
                                   self.xbf[:, j * 512:(j + 1) * 512], psB[:, 0:512])
            self.inc(ins, "v")
        self.V(fv2)
        vw2 = self.cnt["v"]
        self.g_qkv = [("v", vw2), ("v", vw2)]

        for fi in range(24):
            pb = fi % 2
            P = self.P_qkv[pb]
            gn, gv = self.g_qkv[pb]

            def ft3(t, fi=fi, P=P, gn=gn, gv=gv):
                self.wait(t, "lg", self.lg0_total)
                self.wait(t, "v", vw2)
                self.wait(t, gn, gv)
                for dj in range(8):
                    ins = t.matmul(P[:],
                                   self.wqkv_sb[:, dj * 3072 + fi * 128: dj * 3072 + (fi + 1) * 128],
                                   self.ln1x[:, dj * 512:(dj + 1) * 512],
                                   start=(dj == 0), stop=(dj == 7))
                self.inc(ins, "t")
            self.T(ft3)
            twf = self.cnt["t"]

            def fv3(v, fi=fi, P=P, twf=twf):
                self.wait(v, "t", twf)
                for hh in range(2):
                    cs = hh * 256
                    if fi < 16:
                        a = fi
                        p = a % 2
                        c0 = 2048 * p + (a // 2)
                        dst = self.qk_lo[hh][:, c0:c0 + 8 * 255 + 1:8]
                        ins = v.tensor_scalar(dst, P[:, cs:cs + 256],
                                              self.biasp[:, fi:fi + 1], self.qsc[:],
                                              ALU.add, ALU.mult)
                    else:
                        j = fi - 16
                        dst = self.v2[hh][:, j:j + 8 * 255 + 1:8]
                        ins = v.tensor_scalar(dst, P[:, cs:cs + 256],
                                              self.biasp[:, fi:fi + 1], None, ALU.add)
                self.inc(ins, "v")
            self.V(fv3)
            self.g_qkv[pb] = ("v", self.cnt["v"])

        gather_done = self.cnt["v"]
        if hp == 0:
            self.dbg("qk0", self.qk_lo[0][:], (128, SUB), BF16)
            self.dbg("v20", self.v2[0][:], (128, 2048), F32)
            self.dbg("ln1x0", self.ln1x[:], (128, 8 * 512), BF16)

        self._head_setup(0, gather_done)
        self._negm_pre(1, gather_done)
        self._head_attn(0)
        self._head_setup(1, gather_done, skip_negm=True)
        self._head_ln2(hp, 0)
        self._head_attn(1)
        self._head_ln2(hp, 1)

    def _negm_pre(self, hh, gather_done):
        qk_lo = self.qk_lo[hh]

        def fg(g):
            self.wait(g, "v", gather_done)
            g.reduce_max(self.red[0:64, 3:4], qk_lo[0:64, :], axis=AX.X, apply_absolute_value=True)
            self.inc(g.reduce_max(self.red[0:64, 4:5], qk_lo[64:128, :], axis=AX.X,
                                  apply_absolute_value=True), "v")
            g.wait_ge(self.sem["v"], self.cnt["v"])
            self.inc(g.tensor_mul(self.red[0:64, 5:6], self.red[0:64, 3:4],
                                  self.red[0:64, 4:5]), "v")
        self.V(fg)
        gw = self.cnt["v"]

        def mid():
            PQ = self.P_qkv[1]
            gq_n, gq_v = self.g_qkv[1]

            def ftm(t, gq_n=gq_n, gq_v=gq_v):
                self.wait(t, "v", gw)
                self.wait(t, gq_n, gq_v)
                self.inc(t.matmul(PQ[0:1, 0:1], self.ones_col[0:64, :],
                                  self.red[0:64, 5:6], start=True, stop=True), "t")
            self.T(ftm)
            twm = self.cnt["t"]

            def fvm(v, twm=twm):
                self.wait(v, "t", twm)
                self.inc(v.tensor_copy(self.red[0:1, 7:8], PQ[0:1, 0:1]), "v")
            self.V(fvm)
            vwm = self.cnt["v"]

            def ft(t, vwm=vwm):
                self.wait(t, "v", vwm)
                self.inc(t.matmul(PQ[:, 0:1], self.negsc[:], self.red[0:1, 7:8],
                                  start=True, stop=True), "t")
            self.T(ft)
            tw = self.cnt["t"]
            gnm_n, gnm_v = self.g_negM

            def fv(v, tw=tw, gnm_n=gnm_n, gnm_v=gnm_v, hh=hh):
                self.wait(v, "t", tw)
                self.wait(v, gnm_n, gnm_v)
                self.inc(v.tensor_copy(self.negM[:, hh:hh + 1], PQ[:, 0:1]), "v")
            self.V(fv)
            self.g_qkv[1] = ("v", self.cnt["v"])
            self.negm_pre_ready = self.cnt["v"]
        self.mid_emit.append(mid)

    def _head_setup(self, hh, gather_done, skip_negm=False):
        qk_lo, qk_hi, v2, vaug = self.qk_lo[hh], self.qk_hi[hh], self.v2[hh], self.vaug[hh]

        du = f"du{hh}"
        gh_n, gh_v = self.g_qkhi

        def fy(sy):
            self.wait(sy, "v", gather_done)
            self.wait(sy, gh_n, gh_v)
            self.inc(sy.dma_start(out=qk_hi[64:128, :], in_=qk_lo[0:64, :]), du, 16)
            self.inc(sy.dma_start(out=qk_hi[0:64, :], in_=qk_lo[64:128, :]), du, 16)
        self.Y(fy)
        self.dup_ready = (du, self.cnt[du])

        if skip_negm:
            tprev = self.g_qkhi

            def fvz(v, tprev=tprev):
                self.wait(v, tprev[0], tprev[1])
                v.memset(vaug[:], 0.0)
                self.inc(v.memset(vaug[:, 64:32 * 128:128], 1.0), "v")
            self.V(fvz)
            self.negM_ready = self.negm_pre_ready
        else:
            def fg(g):
                self.wait(g, "v", gather_done)
                g.reduce_max(self.red[0:64, 0:1], qk_lo[0:64, :], axis=AX.X, apply_absolute_value=True)
                self.inc(g.reduce_max(self.red[0:64, 1:2], qk_lo[64:128, :], axis=AX.X,
                                      apply_absolute_value=True), "v")
                g.wait_ge(self.sem["v"], self.cnt["v"])
                self.inc(g.tensor_mul(self.red[0:64, 2:3], self.red[0:64, 0:1],
                                      self.red[0:64, 1:2]), "v")
            self.V(fg)
            gw = self.cnt["v"]
            gm_n, gm_v = self.g_misc

            def ftm(t):
                self.wait(t, "v", gw)
                self.wait(t, gm_n, gm_v)
                self.inc(t.matmul(self.P_misc[0:1, 0:1], self.ones_col[0:64, :],
                                  self.red[0:64, 2:3], start=True, stop=True), "t")
            self.T(ftm)
            twm = self.cnt["t"]

            def fvm(v):
                self.wait(v, "t", twm)
                self.inc(v.tensor_copy(self.red[0:1, 6:7], self.P_misc[0:1, 0:1]), "v")
            self.V(fvm)
            vwm = self.cnt["v"]

            def ft(t):
                self.wait(t, "v", vwm)
                self.inc(t.matmul(self.P_misc[:, 0:1], self.negsc[:], self.red[0:1, 6:7],
                                  start=True, stop=True), "t")
            self.T(ft)
            tw = self.cnt["t"]
            gnm_n, gnm_v = self.g_negM

            def fv(v, hh=hh):
                self.wait(v, "t", tw)
                self.wait(v, gnm_n, gnm_v)
                v.memset(vaug[:], 0.0)
                v.memset(vaug[:, 64:32 * 128:128], 1.0)
                self.inc(v.tensor_copy(self.negM[:, hh:hh + 1], self.P_misc[:, 0:1]), "v")
            self.V(fv)
            self.g_misc = ("v", self.cnt["v"])
            self.negM_ready = self.cnt["v"]

        for grp in range(4):
            gO_n, gO_v = self.g_O

            def ft2(t, grp=grp, gO_n=gO_n, gO_v=gO_v):
                self.wait(t, "v", gather_done)
                self.wait(t, gO_n, gO_v)
                for ib in range(8):
                    ci = grp * 8 + ib
                    par, ki = ci // 16, ci % 16
                    if par == 0:
                        ins = t.transpose(self.P_O[:, 64 * ib:64 * ib + 64],
                                          v2[0:64, ki * 128:(ki + 1) * 128],
                                          self.ident[0:64, 0:64])
                    else:
                        ins = t.transpose(self.P_O[:, 64 * ib:64 * ib + 64],
                                          v2[64:128, ki * 128:(ki + 1) * 128],
                                          self.ident[64:128, 64:128])
                self.inc(ins, "t")
            self.T(ft2)
            twk = self.cnt["t"]

            def fv2(v, grp=grp, twk=twk):
                self.wait(v, "t", twk)
                dst = vaug[:].rearrange("p (cc w) -> p cc w", w=128)[:, grp * 8:grp * 8 + 8, 0:64]
                srcv = self.P_O[:].rearrange("p (cc w) -> p cc w", w=64)
                self.inc(v.tensor_copy(dst, srcv), "v")
            self.V(fv2)
            self.g_O = ("v", self.cnt["v"])
        self.vaug_ready = self.cnt["v"]

    def _head_attn(self, hh):
        qk_lo, qk_hi, vaug = self.qk_lo[hh], self.qk_hi[hh], self.vaug[hh]
        negM_ready = self.negM_ready
        dup_ready = self.dup_ready
        vaug_ready = self.vaug_ready
        nM = self.negM[:, hh:hh + 1]

        pend_epi = [None]
        qtpar = [0]

        def emit_epi(pq, ut):
            two = self.cnt["t"]
            par = qtpar[0] % 2
            ob = self.oraw[0:64, 512 * par:512 * par + 512]
            zrow = 64 + 32 * par
            zb_ = self.oraw[zrow:zrow + 1, 0:1024].bitcast(F32)
            gor_n, gor_v = self.g_oraw[par]
            qtpar[0] += 1

            def fve(v, two=two, ob=ob, zb_=zb_, gor_n=gor_n, gor_v=gor_v):
                self.wait(v, "t", two)
                self.wait(v, gor_n, gor_v)
                v.tensor_copy(ob, self.P_O[0:64, 0:512])
                self.inc(v.tensor_copy(zb_, self.P_O[64:65, 0:512]), "v")
            self.V(fve)
            self.g_O = ("v", self.cnt["v"])

            def fvr(v, zb_=zb_):
                self.inc(v.reciprocal(self.zirow[0:1, 0:512], zb_), "v")
            self.V(fvr)
            vwz = self.cnt["v"]

            def fin(pq=pq, ut=ut, ob=ob, zb_=zb_, par=par, vwz=vwz):
                gm_n, gm_v = self.g_misc

                def ft3(t, vwz=vwz, gm_n=gm_n, gm_v=gm_v):
                    self.wait(t, "v", vwz)
                    self.wait(t, gm_n, gm_v)
                    self.inc(t.matmul(self.P_misc[0:64, 0:512], self.ones_row[0:1, 0:64],
                                      self.zirow[0:1, 0:512], start=True, stop=True), "t")
                self.T(ft3)
                twz = self.cnt["t"]

                def fv3(v, twz=twz, pq=pq, ut=ut, ob=ob):
                    self.wait(v, "t", twz)
                    v.tensor_mul(ob, ob, self.P_misc[0:64, 0:512])
                    for ia in range(8):
                        a2 = 2 * ia + pq
                        r0 = (a2 % 2) * 64
                        dst = self.attn_bf[r0:r0 + 64,
                                           (a2 // 2) * 256 + 64 * ut:(a2 // 2) * 256 + 64 * (ut + 1)]
                        ins = v.tensor_copy(dst, ob[0:64, ia:ia + 8 * 63 + 1:8])
                    self.inc(ins, "v")
                self.V(fv3)
                self.g_misc = ("v", self.cnt["v"])
                self.g_oraw[par] = ("v", self.cnt["v"])
            pend_epi[0] = fin

        pend = [None]
        pend_tile = [None]

        def emit_av(uk, sws, mask_done, nuk):
            gO_n, gO_v = self.g_O

            def ft2(t, uk=uk, sws=sws, mask_done=mask_done, nuk=nuk,
                    gO_n=gO_n, gO_v=gO_v):
                ET = self.etile[uk % 2]
                if mask_done is not None:
                    self.wait(t, "v", mask_done)
                else:
                    self.wait(t, "s", sws)
                if uk == 0:
                    self.wait(t, gO_n, gO_v)
                    self.wait(t, "v", vaug_ready)
                t.matmul(self.P_O[0:128, 0:512], vaug[:, uk * 128:(uk + 1) * 128],
                         ET[:, 0:512], start=(uk == 0), stop=False,
                         skip_group_check=True)
                ins = t.matmul(self.P_O[0:128, 0:512],
                               vaug[:, (16 + uk) * 128:(17 + uk) * 128],
                               ET[:, 512:1024], start=False, stop=(uk == nuk - 1),
                               skip_group_check=True)
                self.inc(ins, "t")
            self.T(ft2)
            self.g_etile[uk % 2] = ("t", self.cnt["t"])

        def flush_pend():
            emit_av(*pend[0])
            pend[0] = None
            if pend_tile[0] is not None:
                ppq, put = pend_tile[0]
                pend_tile[0] = None
                if pend_epi[0] is not None:
                    pend_epi[0]()
                    pend_epi[0] = None
                emit_epi(ppq, put)

        for qt in range(8):
            pq, ut = qt % 2, qt // 2
            q0 = 2048 * pq + 512 * ut
            nuk = 4 * (ut + 1)
            if qt == 2 and self.mid_emit:
                for f in self.mid_emit:
                    f()
                self.mid_emit = []

            for uk in range(nuk):
                sp = uk % 2
                PS = self.P_S[sp]
                ET = self.etile[sp]
                gS_n, gS_v = self.g_S[sp]

                def ft(t, uk=uk, PS=PS, q0=q0, gS_n=gS_n, gS_v=gS_v, first=(uk == 0 and qt == 0)):
                    if first:
                        self.wait(t, dup_ready[0], dup_ready[1])
                        self.wait(t, "v", negM_ready)
                    self.wait(t, gS_n, gS_v)
                    t.matmul(PS[:, 0:512], qk_hi[0:64, 128 * uk:128 * (uk + 1)],
                             qk_lo[0:64, q0:q0 + 512], start=True, stop=True,
                             tile_position=(0, 0), skip_group_check=True)
                    ins = t.matmul(PS[:, 512:1024],
                                   qk_lo[64:128, 2048 + 128 * uk:2048 + 128 * (uk + 1)],
                                   qk_hi[64:128, q0:q0 + 512], start=True, stop=True,
                                   tile_position=(64, 0), skip_group_check=True)
                    self.inc(ins, "t")
                self.T(ft)
                tws = self.cnt["t"]
                gE_n, gE_v = self.g_etile[sp]

                def fs(s, PS=PS, ET=ET, tws=tws, gE_n=gE_n, gE_v=gE_v):
                    self.wait(s, "t", tws)
                    self.wait(s, gE_n, gE_v)
                    self.inc(s.activation(ET[:], PS[:], AF.Exp, bias=nM), "s")
                self.S(fs)
                self.g_S[sp] = ("s", self.cnt["s"])
                sws = self.cnt["s"]

                off = uk * 128 - 512 * ut
                need = []
                for pk in range(2):
                    strict = (pk == 1 and pq == 0)
                    o = off + (1 if strict else 0)
                    if -127 <= o <= 511:
                        need.append((pk, o))
                mask_done = None
                if need:
                    def fv(v, ET=ET, need=tuple(need), sws=sws):
                        self.wait(v, "s", sws)
                        for pk, o in need:
                            start_ = MC - o
                            ins = v.tensor_mul(ET[:, pk * 512:(pk + 1) * 512],
                                               ET[:, pk * 512:(pk + 1) * 512],
                                               self.mmask[:, start_:start_ + 512])
                        self.inc(ins, "v")
                    self.V(fv)
                    mask_done = self.cnt["v"]

                if pend[0] is not None:
                    flush_pend()
                pend[0] = (uk, sws, mask_done, nuk)
            pend_tile[0] = (pq, ut)
        flush_pend()
        if pend_epi[0] is not None:
            pend_epi[0]()
            pend_epi[0] = None
        self.g_negM = ("s", self.cnt["s"])
        self.g_qkhi = ("t", self.cnt["t"])

    def _head_ln2(self, hp, hh):
        h = 2 * hp + hh
        hc = h * 256
        if h == 0:
            self.dbg("attn0", self.attn_bf[:], (128, 8 * 256), BF16)
        vw = self.cnt["v"]

        def fv(v):
            for j in range(8):
                ins = v.tensor_mul(self.x2bf[:, j * 256:(j + 1) * 256],
                                   self.attn_bf[:, j * 256:(j + 1) * 256],
                                   self.attn_bf[:, j * 256:(j + 1) * 256])
            self.inc(ins, "v")
        self.V(fv)
        vw2 = self.cnt["v"]
        gO_n, gO_v = self.g_O

        def ft(t):
            self.wait(t, "v", vw2)
            self.wait(t, gO_n, gO_v)
            for j in range(8):
                t.matmul(self.P_O[:, 0:256], self.ones_bf[:], self.attn_bf[:, j * 256:(j + 1) * 256],
                         start=(j == 0), stop=(j == 7))
            for j in range(8):
                ins = t.matmul(self.P_O[:, 256:512], self.ones_bf[:], self.x2bf[:, j * 256:(j + 1) * 256],
                               start=(j == 0), stop=(j == 7))
            self.inc(ins, "t")
        self.T(ft)
        tw = self.cnt["t"]

        def fsx(v):
            self.wait(v, "t", tw)
            v.tensor_scalar(self.mrow[0:1, 0:256], self.P_O[0:1, 0:256], 1.0 / D, None, ALU.mult)
            self.inc(v.tensor_scalar(self.vrow[0:1, 0:256], self.P_O[0:1, 256:512], 1.0 / D, None, ALU.mult), "v")
        self.V(fsx)
        sw = self._ln_rows(256)
        gm_n, gm_v = self.g_misc

        def ft2(t):
            self.wait(t, "s", sw)
            self.wait(t, gm_n, gm_v)
            self.inc(t.matmul(self.P_misc[:, 0:256], self.ones_row[:], self.rrow[0:1, 0:256],
                              start=True, stop=True), "t")
        self.T(ft2)
        tw2 = self.cnt["t"]

        def fv2(v):
            self.wait(v, "t", tw2)
            for j in range(8):
                xs = self.X[:, j * TOK + hc: j * TOK + hc + 256]
                v.scalar_tensor_tensor(self.lnt[:, 0:256], self.P_O[:, 0:256], -1.0 / D,
                                       self.attn_bf[:, j * 256:(j + 1) * 256], ALU.mult, ALU.add)
                v.tensor_mul(self.lnt[:, 0:256], self.lnt[:, 0:256], self.P_misc[:, 0:256])
                ins = v.tensor_add(xs, xs, self.lnt[:, 0:256])
            self.inc(ins, "v")
        self.V(fv2)
        self.g_misc = ("v", self.cnt["v"])
        self.g_O = ("v", self.cnt["v"])
        vwsp = self.cnt["v"]

        def fgs(g, vwsp=vwsp, hc=hc):
            self.wait(g, "v", vwsp)
            self.inc(g.dma_start(
                out=self.xmid_d.ap().rearrange("j p t -> p j t")[:, :, hc:hc + 256],
                in_=self.X[:].rearrange("p (j t) -> p j t", j=8)[:, :, hc:hc + 256]),
                "xsp", 16)
        self.G(fgs)

    def _ffn_alloc(self):
        self.xc = [self.sb(f"xc{i}", [128, 8 * 512], F32) for i in range(2)]
        self.g_xc = [("v", 0), ("v", 0)]
        self.sK = self.sb("sK", [128, 8], F32)
        self.sKb = self.sb("sKb", [128, 8], BF16)
        self.sKr = self.sb("sKr", [128, 8], BF16)
        self.sKi = self.sb("sKi", [128, 8], BF16)
        self.carry_f = self.sb("carry_f", [128, 4], F32)
        self.carry_i = self.sb("carry_i", [128, 4], F32)
        self.s_ag = self.sb("s_ag", [2, 1024], BF16)
        self.qc = self.sb("qc", [128, 8 * 512], BF16)
        self.kc = self.sb("kc", [128, 8 * 512], BF16)
        self.iqc = self.sb("iqc", [128, 8 * 512], BF16)
        self.ikc = self.sb("ikc", [128, 8 * 512], BF16)
        self.hb = self.sb("hb", [128, 8 * 512], BF16)
        self.ofwd = self.sb("ofwd", [128, 8 * 512], BF16)
        self.xbf2 = self.qc
        self.x2bf2 = self.kc
        self.ln3c = self.sb("ln3c", [128, 8 * 512], BF16)
        self.cmb = self.hb
        self.fout = self.iqc
        self.eexp_bf = [self.sb("eexpb_f", [8, 512], BF16), self.sb("eexpb_i", [8, 512], BF16)]
        self.lrow = self.sb("lrow", [8, 512], F32)
        self.etm = self.sb("etm", [128, 32], F32)
        self.r1 = [self.sb(f"r1_{i}", [128, 1], F32) for i in range(4)]
        self.prod2 = self.sb("prod2", [128, 4 * 512], BF16)
        self.cums = self.sb("cums", [128, 512], F32)
        self.scr = self.sb("scr", [128, 512], F32)
        self.scr2 = self.sb("scr2", [128, 1], F32)
        self.znb = self.sb("znb", [128, 512], F32)
        self.sel = self.sb("sel", [8, 512], BF16)
        self.ninv_r = [self.sb(f"ninv_r{i}", [1, 512], F32) for i in range(2)]
        self.mskb = self.sb("mskb", [2, 1], BF16)
        self.b2pe = self.sb("b2pe", [8, 1], F32)
        self.tmp8 = self.sb("tmp8", [8, 1], F32)
        self.rw2 = self.sb("rw2", [128, 64], BF16)
        self.bf_b = self.sb("bf_b", [8, 1024], BF16)
        self.bi_b = self.sb("bi_b", [8, 1024], BF16)
        self.rb1 = self.sb("rb1", [128, 8], F32)
        self.wbig = [self.sb(f"wbig{i}", [128, 8 * 1024], BF16) for i in range(3)]
        self.wsm = [self.sb(f"wsm{i}", [128, 8 * 512], BF16) for i in range(4)]
        self.Q = [self.ps(f"Q{i}", [128, 512], F32) for i in range(8)]
        self.gQ = [("v", 0)] * 8
        self.bo_extra = []
        self._arena_init()
        bar = self.ffn_bar

        def fy(sy):
            self.wait(sy, bar[0], bar[1])
            self.inc(sy.dma_start(out=self.sel[:], in_=self.dr["sel"]), "lf", 16)
            self.inc(sy.dma_start(out=self.b2pe[:], in_=self.dr["b2"]), "lf", 16)
            self.inc(sy.dma_start(out=self.tmp8[:], in_=self.dr["eb"]), "lf", 16)
            self.inc(sy.dma_start(out=self.rb1[:], in_=self.dr["rb1"]), "lf", 16)
        self.Y(fy)

        def fg(g):
            self.wait(g, bar[0], bar[1])
            self.wait(g, "xsp", 16)
            self.inc(g.dma_start(out=self.mskb[:], in_=self.dr["mskb"]), "lgc", 16)
            self.inc(g.dma_start(out=self.rw2[:].rearrange("p (j e) -> p j e", j=8),
                                 in_=self.dr["rw2_t"].rearrange("j p e -> p j e")), "lgc", 16)
            self.inc(g.dma_start(out=self.bf_b[:], in_=self.dr["b_fwd"]), "lgc", 16)
            self.inc(g.dma_start(out=self.bi_b[:], in_=self.dr["b_inv"]), "lgc", 16)
        self.G(fg)
        self.lf_total = self.cnt["lf"]
        self.lgc0 = self.cnt["lgc"]
        lfw = self.lf_total

        def fv(v):
            self.wait(v, "lf", lfw)
            self.inc(v.tensor_add(self.b2pe[:], self.b2pe[:], self.tmp8[:]), "v")
        self.V(fv)
        self.ffn_misc_v = self.cnt["v"]

    def _xc_load(self, c, slot):
        gn, gv = self.g_xc[slot]
        semn = f"xc{slot}"

        def fy(sy):
            self.wait(sy, "xsp", min(32 * (c + 1), 128))
            self.wait(sy, self.ffn_bar[0], self.ffn_bar[1])
            self.wait(sy, gn, gv)
            col = c * CH
            sy.dma_start(
                out=self.xc[slot][:].rearrange("p (j t) -> p j t", j=8),
                in_=self.xmid_d.ap().rearrange("j p t -> p j t")[:, :, col:col + 512]).then_inc(self.sem[semn], 16)
            self.cnt[semn] += 16
            self.inc(sy.dma_start(out=self.ninv_r[slot][:],
                                  in_=self.dr["ninv"][0:1, col:col + 512]), semn, 16)
        self.Y(fy)
        return (semn, self.cnt[semn])

    def _ln_sq(self, xcs, xc_ready, fv_extra=()):
        def fv(v):
            self.wait(v, xc_ready[0], xc_ready[1])
            for nm, val in fv_extra:
                self.wait(v, nm, val)
            for j in range(8):
                xs = xcs[:, j * 512:(j + 1) * 512]
                v.tensor_copy(self.xbf2[:, j * 512:(j + 1) * 512], xs)
            for j in range(8):
                ins = v.tensor_mul(self.x2bf2[:, j * 512:(j + 1) * 512],
                                   self.xbf2[:, j * 512:(j + 1) * 512],
                                   self.xbf2[:, j * 512:(j + 1) * 512])
            self.inc(ins, "v")
        self.V(fv)
        return self.cnt["v"]

    def _ln_stats(self, vw, banks):
        psA, psB = self.Q[banks[0]], self.Q[banks[1]]
        (gn0, gv0), (gn1, gv1) = self.gQ[banks[0]], self.gQ[banks[1]]

        def ft(t):
            self.wait(t, "v", vw)
            self.wait(t, gn0, gv0)
            self.wait(t, gn1, gv1)
            for j in range(8):
                t.matmul(psA[:], self.ones_bf[:], self.xbf2[:, j * 512:(j + 1) * 512],
                         start=(j == 0), stop=(j == 7))
            for j in range(8):
                ins = t.matmul(psB[:], self.ones_bf[:], self.x2bf2[:, j * 512:(j + 1) * 512],
                               start=(j == 0), stop=(j == 7))
            self.inc(ins, "t")
        self.T(ft)
        tw = self.cnt["t"]

        def fsx(v):
            self.wait(v, "t", tw)
            v.tensor_scalar(self.mrow[0:1, 0:512], psA[0:1, :], 1.0 / D, None, ALU.mult)
            self.inc(v.tensor_scalar(self.vrow[0:1, 0:512], psB[0:1, :], 1.0 / D, None, ALU.mult), "v")
        self.V(fsx)
        sw = self._ln_rows(512)
        self.last_ln_sw = sw
        return sw, tw

    def _ln_apply(self, xcs, dst_bf, sw, banks, accum=False, fv3_extra=()):
        psA, psB = self.Q[banks[0]], self.Q[banks[1]]

        def ft2(t):
            self.wait(t, "s", sw)
            self.inc(t.matmul(psB[:], self.ones_row[:], self.rrow[0:1, 0:512],
                              start=True, stop=True), "t")
        self.T(ft2)
        tw2 = self.cnt["t"]

        def fv3(v):
            self.wait(v, "t", tw2)
            for nm, val in fv3_extra:
                self.wait(v, nm, val)
            for j in range(8):
                xs = xcs[:, j * 512:(j + 1) * 512]
                v.scalar_tensor_tensor(self.scr[:], psA[:], -1.0 / D, xs, ALU.mult, ALU.add)
                if accum:
                    self.inc(v.scalar_tensor_tensor(dst_bf[:, j * 512:(j + 1) * 512],
                                                    self.scr[:], 1.0, psB[:], ALU.mult, ALU.mult,
                                                    accum_out=self.scr2[:, 0:1]), "v")
                    v.wait_ge(self.sem["v"], self.cnt["v"])
                    ins = v.tensor_add(self.sK[:, j:j + 1], self.sK[:, j:j + 1],
                                       self.scr2[:, 0:1])
                else:
                    ins = v.scalar_tensor_tensor(dst_bf[:, j * 512:(j + 1) * 512],
                                                 self.scr[:], 1.0, psB[:], ALU.mult, ALU.mult)
            self.inc(ins, "v")
        self.V(fv3)
        self.gQ[banks[0]] = ("v", self.cnt["v"])
        self.gQ[banks[1]] = ("v", self.cnt["v"])
        return self.cnt["v"]

    def _ln_chunk(self, xcs, xc_ready, dst_bf, accum=False, banks=(0, 1),
                  fv_extra=(), fv3_extra=()):
        vw = self._ln_sq(xcs, xc_ready, fv_extra)
        sw, tw = self._ln_stats(vw, banks)
        av = self._ln_apply(xcs, dst_bf, sw, banks, accum, fv3_extra)
        return av, tw

    def _ffn_ln3_all(self):
        def fvz(v):
            self.inc(v.memset(self.sK[:], 0.0), "v")
        self.V(fvz)

        class ThrowDst:
            def __init__(s2, buf):
                s2.b = buf

            def __getitem__(s2, key):
                return s2.b[:]
        tdst = ThrowDst(self.znb)
        for c in range(NCH):
            slot = c % 2
            rdy = self._xc_load(c, slot)
            self._ln_chunk(self.xc[slot][:], rdy, tdst, accum=True)
            self.g_xc[slot] = ("v", self.cnt["v"])
        self.sK_done = self.cnt["v"]

    def _arena_init(self):
        bar = ("v", self.cnt["v"])
        self.ffn_bar = bar
        self.ar = {
            "big": dict(bufs=self.wbig, n=len(self.wbig), i=0,
                        freed=[True] * len(self.wbig),
                        fcnt=[bar] * len(self.wbig),
                        slot={}, lg={}, pending=[], live=set()),
            "small": dict(bufs=self.wsm, n=len(self.wsm), i=0,
                          freed=[True] * len(self.wsm),
                          fcnt=[bar] * len(self.wsm),
                          slot={}, lg={}, pending=[], live=set()),
        }

    def _ar_emit(self, cls, name):
        a = self.ar[cls]
        slot = a["i"] % a["n"]
        assert a["freed"][slot], (cls, name, slot)
        a["i"] += 1
        fn, fv = a["fcnt"][slot]
        buf = a["bufs"][slot]
        semn = ("lb" if cls == "big" else "ls") + str(slot)

        def fg(g):
            self.wait(g, "xsp", 16)
            self.wait(g, self.ffn_bar[0], self.ffn_bar[1])
            self.wait(g, fn, fv)
            nj = self.dr[name].shape[0]
            self.inc(g.dma_start(out=buf[:].rearrange("p (j d) -> p j d", j=nj),
                                 in_=self.dr[name].rearrange("j p d -> p j d")), semn, 16)
        self.G(fg)
        a["slot"][name] = slot
        a["lg"][name] = (semn, self.cnt[semn])
        a["freed"][slot] = False
        a["live"].add(name)

    def _ar_try(self, cls):
        a = self.ar[cls]
        while a["pending"] and a["freed"][a["i"] % a["n"]]:
            self._ar_emit(cls, a["pending"].pop(0))

    def _ar_req(self, cls, names):
        self.ar[cls]["pending"] += names
        self._ar_try(cls)

    def _ar_free(self, cls, name):
        a = self.ar[cls]
        slot = a["slot"][name]
        assert name in a["live"], (cls, name)
        a["live"].discard(name)
        a["freed"][slot] = True
        a["fcnt"][slot] = ("t", self.cnt["t"])
        self._ar_try(cls)

    def _ar_ap(self, cls, name):
        a = self.ar[cls]
        assert name in a["live"], (cls, name, "not loaded at emission point")
        return a["bufs"][a["slot"][name]], a["lg"][name]

    def _proj_dd(self, wbuf, src8, dst_ps, waits):
        def ft(t):
            for nm, val in waits:
                self.wait(t, nm, val)
            for di in range(8):
                for dj in range(8):
                    ins = t.matmul(dst_ps[:, di:di + 1],
                                   wbuf[:, dj * 1024 + di * 128: dj * 1024 + (di + 1) * 128],
                                   src8[:, dj:dj + 1], start=(dj == 0), stop=(dj == 7))
            self.inc(ins, "t")
        self.T(ft)
        return self.cnt["t"]

    def _ffn_exchange(self):
        self._ar_emit("big", "wk_t")
        self._ar_emit("big", "winv_t")
        self._ar_emit("small", "wf_r")
        self._ar_emit("small", "vi_r")
        lg_wk = self.ar["big"]["lg"]["wk_t"]
        lg_winv = self.ar["big"]["lg"]["winv_t"]
        lg_wf = self.ar["small"]["lg"]["wf_r"]
        lg_vi = self.ar["small"]["lg"]["vi_r"]
        self._ar_req("big", ["wq_t", "rw1_t", "wo_t"]
                     + ["wk_t", "wq_t", "winv_t", "rw1_t", "wo_t"] * (NCH - 1))
        self._ar_req("small", ["vf_r", "uf_r", "wie_r", "ui_r"]
                      + ["wf_r", "vf_r", "uf_r", "wie_r", "vi_r", "ui_r"] * (NCH - 1))
        sK_done = self.sK_done

        def fvc(v):
            self.wait(v, "v", sK_done)
            self.inc(v.tensor_copy(self.sKb[:], self.sK[:]), "v")
        self.V(fvc)
        vwc = self.cnt["v"]

        gn, gv = self.gQ[7]
        tw = self._proj_dd(self.wbig[0], self.sKb, self.Q[7],
                           [lg_wk, ("v", vwc), (gn, gv)])

        def fv2(v):
            self.wait(v, "t", tw)
            self.inc(v.tensor_copy(self.scr[:, 0:8], self.Q[7][:, 0:8]), "v")
        self.V(fv2)
        self.gQ[7] = ("v", self.cnt["v"])
        vw2 = self.cnt["v"]

        def fg(g):
            self.wait(g, "v", vw2)
            self.inc(g.dma_start(out=self.cc_in.ap().rearrange("o (p j) -> p (o j)", j=8),
                                 in_=self.scr[:, 0:8]), "ccd", 16)
            self.wait(g, "ccd", self.cnt["ccd"])
            groups = [[2 * i, 2 * i + 1] for i in range(max(1, self.n_cores // 2))]
            self.inc(g.collective_compute("AllGather", ALU.bypass, replica_groups=groups,
                                          ins=[self.cc_in.ap().opt()],
                                          outs=[self.cc_out.ap().opt()]), "cc")
            self.wait(g, "cc", self.cnt["cc"])
            self.inc(g.dma_start(out=self.s_ag[:], in_=self.cc_out.ap()), "ccd", 16)
        self.G(fg)
        self.exch_lg2 = self.cnt["ccd"]

    def _exchange_tail(self):
        lg2 = self.exch_lg2
        lg_winv = self.ar["big"]["lg"]["winv_t"]
        lg_wf = self.ar["small"]["lg"]["wf_r"]
        lg_vi = self.ar["small"]["lg"]["vi_r"]
        gn6, gv6 = self.gQ[6]

        def ft2(t):
            self.wait(t, "ccd", lg2)
            self.wait(t, "lgc", self.lgc0)
            self.wait(t, gn6, gv6)
            for dj in range(8):
                ins = t.matmul(self.Q[6][:, dj:dj + 1], self.s_ag[:, dj:dj + 8 * 127 + 1:8],
                               self.mskb[:], start=True, stop=True)
            self.inc(ins, "t")
        self.T(ft2)
        tw2 = self.cnt["t"]

        def fv3(v):
            self.wait(v, "t", tw2)
            self.inc(v.tensor_copy(self.sKr[:], self.Q[6][:, 0:8]), "v")
        self.V(fv3)
        self.gQ[6] = ("v", self.cnt["v"])
        sKr_v = self.cnt["v"]

        gn7, gv7 = self.gQ[7]
        tw3 = self._proj_dd(self.wbig[1], self.sKr, self.Q[7],
                            [lg_winv, ("v", sKr_v), (gn7, gv7)])

        def fv4(v):
            self.wait(v, "t", tw3)
            self.inc(v.tensor_copy(self.sKi[:], self.Q[7][:, 0:8]), "v")
        self.V(fv4)
        self.gQ[7] = ("v", self.cnt["v"])
        sKi_v = self.cnt["v"]

        gn5, gv5 = self.gQ[5]

        def ft4(t):
            self.wait(t, lg_wf[0], lg_wf[1])
            self.wait(t, "v", sKr_v)
            self.wait(t, gn5, gv5)
            for krc in range(4):
                for dj in range(8):
                    ins = t.matmul(self.Q[5][:, krc:krc + 1],
                                   self.wsm[0][:, dj * 512 + krc * 128: dj * 512 + (krc + 1) * 128],
                                   self.sKr[:, dj:dj + 1], start=(dj == 0), stop=(dj == 7))
            self.inc(ins, "t")
        self.T(ft4)
        tw4 = self.cnt["t"]

        def ft5(t):
            self.wait(t, lg_vi[0], lg_vi[1])
            self.wait(t, "v", sKi_v)
            for krc in range(4):
                for dj in range(8):
                    ins = t.matmul(self.Q[5][:, 4 + krc:5 + krc],
                                   self.wsm[1][:, dj * 512 + krc * 128: dj * 512 + (krc + 1) * 128],
                                   self.sKi[:, dj:dj + 1], start=(dj == 0), stop=(dj == 7))
            self.inc(ins, "t")
        self.T(ft5)
        tw5 = self.cnt["t"]

        def fv5(v):
            self.wait(v, "t", tw5)
            v.tensor_copy(self.carry_f[:], self.Q[5][:, 0:4])
            self.inc(v.tensor_copy(self.carry_i[:], self.Q[5][:, 4:8]), "v")
        self.V(fv5)
        self.gQ[5] = ("v", self.cnt["v"])
        self.carry_ready = self.cnt["v"]
        self.dbg("skr", self.sKr[:], (128, 8), BF16)
        self.dbg("cf", self.carry_f[:], (128, 4), F32)
        self.dbg("ci2", self.carry_i[:], (128, 4), F32)

    def _mm1024(self, wname, src, dst, waits, evac="v", act=None, bias=None,
                dst_extra=None, pairs=((0, 1), (2, 3))):
        wbuf, lg = self._ar_ap("big", wname)
        lgn, lgv = lg
        for dp in range(4):
            b0, b1 = pairs[dp % 2]
            P0, P1 = self.Q[b0], self.Q[b1]
            (gn0, gv0), (gn1, gv1) = self.gQ[b0], self.gQ[b1]
            di0, di1 = 2 * dp, 2 * dp + 1

            def ft(t, di0=di0, di1=di1, P0=P0, P1=P1, gn0=gn0, gv0=gv0, gn1=gn1, gv1=gv1):
                self.wait(t, lgn, lgv)
                for nm, val in waits:
                    self.wait(t, nm, val)
                self.wait(t, gn0, gv0)
                self.wait(t, gn1, gv1)
                for dj in range(8):
                    t.matmul(P0[:],
                             wbuf[:, dj * 1024 + di0 * 128: dj * 1024 + (di0 + 1) * 128],
                             src[:, dj * 512:(dj + 1) * 512],
                             start=(dj == 0), stop=(dj == 7))
                    ins = t.matmul(P1[:],
                                   wbuf[:, dj * 1024 + di1 * 128: dj * 1024 + (di1 + 1) * 128],
                                   src[:, dj * 512:(dj + 1) * 512],
                                   start=(dj == 0), stop=(dj == 7))
                self.inc(ins, "t")
            self.T(ft)
            tw = self.cnt["t"]

            if act is not None:
                def fs(s, di0=di0, di1=di1, P0=P0, P1=P1, tw=tw):
                    self.wait(s, "t", tw)
                    b0b = bias[:, di0:di0 + 1] if bias is not None else 0.0
                    b1b = bias[:, di1:di1 + 1] if bias is not None else 0.0
                    s.activation(dst[:, di0 * 512:(di0 + 1) * 512], P0[:], act, bias=b0b)
                    self.inc(s.activation(dst[:, di1 * 512:(di1 + 1) * 512], P1[:], act, bias=b1b), "s")
                self.S(fs)
                self.gQ[b0] = ("s", self.cnt["s"])
                self.gQ[b1] = ("s", self.cnt["s"])
            else:
                def fv(v, di0=di0, di1=di1, P0=P0, P1=P1, tw=tw):
                    self.wait(v, "t", tw)
                    v.tensor_copy(dst[:, di0 * 512:(di0 + 1) * 512], P0[:])
                    self.inc(v.tensor_copy(dst[:, di1 * 512:(di1 + 1) * 512], P1[:]), "v")
                self.V(fv)
                self.gQ[b0] = ("v", self.cnt["v"])
                self.gQ[b1] = ("v", self.cnt["v"])
        return self.cnt["s"] if act is not None else self.cnt["v"]

    def _router(self, br, qsrc, ready):
        hv = self._mm1024("rw1_t", qsrc, self.hb, [ready], act=AF.Gelu, bias=self.rb1)
        gn2, gv2 = self.gQ[4]

        def ft(t):
            self.wait(t, "s", hv)
            self.wait(t, "lgc", self.lgc0)
            self.wait(t, gn2, gv2)
            for dj in range(8):
                ins = t.matmul(self.Q[4][0:8, :], self.rw2[:, dj * 8:(dj + 1) * 8],
                               self.hb[:, dj * 512:(dj + 1) * 512],
                               start=(dj == 0), stop=(dj == 7))
            self.inc(ins, "t")
        self.T(ft)
        tw = self.cnt["t"]

        def fsv(v):
            self.wait(v, "t", tw)
            self.wait(v, "v", self.ffn_misc_v)
            self.inc(v.tensor_scalar_add(self.lrow[:], self.Q[4][0:8, :], self.b2pe[:]), "v")
        self.V(fsv)
        self.gQ[4] = ("v", self.cnt["v"])
        lw = self.cnt["v"]
        gn7, gv7 = self.gQ[5]

        def ft2(t):
            self.wait(t, "v", lw)
            self.wait(t, gn7, gv7)
            for cc in range(4):
                ins = t.transpose(self.Q[5][:, 8 * cc:8 * cc + 8],
                                  self.lrow[0:8, 128 * cc:128 * (cc + 1)],
                                  self.ident[0:8, 0:8])
            self.inc(ins, "t")
        self.T(ft2)
        tw2 = self.cnt["t"]

        def fv3(v):
            self.wait(v, "t", tw2)
            for cc in range(4):
                ins = v.reduce_max(self.r1[cc][:], self.Q[5][:, 8 * cc:8 * cc + 8], axis=AX.X)
            self.inc(ins, "v")
            v.wait_ge(self.sem["v"], self.cnt["v"])
            for cc in range(4):
                ins = v.tensor_scalar(self.r1[cc][:], self.r1[cc][:], -1.0, None, ALU.mult)
            self.inc(ins, "v")
        self.V(fv3)
        vw3 = self.cnt["v"]

        def fs2(s):
            self.wait(s, "v", vw3)
            for cc in range(4):
                ins = s.activation(self.etm[:, 8 * cc:8 * cc + 8],
                                   self.Q[5][:, 8 * cc:8 * cc + 8], AF.Exp,
                                   bias=self.r1[cc][:])
            self.inc(ins, "s")
        self.S(fs2)
        sw2 = self.cnt["s"]

        def fv4(v):
            self.wait(v, "s", sw2)
            for cc in range(4):
                ins = v.reduce_sum(self.r1[cc][:], self.etm[:, 8 * cc:8 * cc + 8], axis=AX.X)
            self.inc(ins, "v")
            v.wait_ge(self.sem["v"], self.cnt["v"])
            for cc in range(4):
                ins = v.reciprocal(self.r1[cc][:], self.r1[cc][:])
            self.inc(ins, "v")
            v.wait_ge(self.sem["v"], self.cnt["v"])
            for cc in range(4):
                ins = v.tensor_scalar(self.etm[:, 8 * cc:8 * cc + 8],
                                      self.etm[:, 8 * cc:8 * cc + 8],
                                      self.r1[cc][:], None, ALU.mult)
            self.inc(ins, "v")
        self.V(fv4)
        self.gQ[5] = ("v", self.cnt["v"])
        vw4 = self.cnt["v"]
        gn2b, gv2b = self.gQ[6]

        def ft3(t):
            self.wait(t, "v", vw4)
            self.wait(t, gn2b, gv2b)
            for cc in range(4):
                ins = t.transpose(self.Q[6][0:8, 128 * cc:128 * (cc + 1)],
                                  self.etm[:, 8 * cc:8 * cc + 8],
                                  self.ident[:, :])
            self.inc(ins, "t")
        self.T(ft3)
        tw3 = self.cnt["t"]

        def fv5(v):
            self.wait(v, "t", tw3)
            self.inc(v.tensor_copy(self.eexp_bf[br][:], self.Q[6][0:8, :]), "v")
        self.V(fv5)
        self.gQ[6] = ("v", self.cnt["v"])
        return self.cnt["v"]

    def _branch_core(self, br, qsrc, ksrc, xv_w, yw_w, q_ready, k_ready, e_ready, carry):
        xbuf, (xln, xlv) = self._ar_ap("small", xv_w)
        ybuf, (yln, ylv) = self._ar_ap("small", yw_w)
        for krc in range(4):
            PX, PY, PE = self.Q[4], self.Q[5], self.Q[6]
            (gn3, gv3), (gn4, gv4), (gn5, gv5) = self.gQ[4], self.gQ[5], self.gQ[6]

            def ft(t, krc=krc, gn3=gn3, gv3=gv3, gn4=gn4, gv4=gv4, gn5=gn5, gv5=gv5):
                self.wait(t, xln, xlv)
                self.wait(t, yln, ylv)
                self.wait(t, "v", max(q_ready, k_ready))
                self.wait(t, gn3, gv3)
                self.wait(t, gn4, gv4)
                for dj in range(8):
                    t.matmul(PX[:], xbuf[:, dj * 512 + krc * 128: dj * 512 + (krc + 1) * 128],
                             qsrc[:, dj * 512:(dj + 1) * 512], start=(dj == 0), stop=(dj == 7))
                for dj in range(8):
                    ins = t.matmul(PY[:], ybuf[:, dj * 512 + krc * 128: dj * 512 + (krc + 1) * 128],
                                   ksrc[:, dj * 512:(dj + 1) * 512], start=(dj == 0), stop=(dj == 7))
                self.inc(ins, "t")
                self.wait(t, "lf", self.lf_total)
                self.wait(t, "v", e_ready)
                self.wait(t, gn5, gv5)
                self.inc(t.matmul(PE[:], self.sel[:, krc * 128:(krc + 1) * 128],
                                  self.eexp_bf[br][:], start=True, stop=True), "t")
            self.T(ft)
            tw = self.cnt["t"]

            def fv(v, krc=krc, tw=tw):
                self.wait(v, "t", tw)
                if krc == 0:
                    self.wait(v, "v", self.carry_ready)
                self.inc(v.tensor_tensor_scan(self.cums[:], self.ones_col[:].broadcast_to([128, 512]),
                                              PY[:], carry[:, krc:krc + 1], ALU.mult, ALU.add), "v")
                v.wait_ge(self.sem["v"], self.cnt["v"])
                v.tensor_copy(carry[:, krc:krc + 1], self.cums[:, 511:512])
                v.tensor_mul(self.scr[:], PX[:], self.cums[:])
                ins = v.tensor_mul(self.prod2[:, krc * 512:(krc + 1) * 512], self.scr[:], PE[:])
                self.inc(ins, "v")
            self.V(fv)
            self.gQ[4] = ("v", self.cnt["v"])
            self.gQ[5] = ("v", self.cnt["v"])
            self.gQ[6] = ("v", self.cnt["v"])
        return self.cnt["v"]

    def _branch_out(self, br, u_w, b_sb, prod_ready, col, out_dst, add_prev=None):
        rdy = self.cur_rdy
        nslot = self.cur_slot
        """out = U.T@prod2 + b.T@eexp_bf; scale by ninv (and alpha for inv);
        write bf16 to out_dst (8 chunks); if add_prev: add ofwd."""
        ubuf, (uln, ulv) = self._ar_ap("small", u_w)
        vwz0 = self.cnt["v"]
        bo_extra = tuple(self.bo_extra)

        def fvz(v, bo_extra=bo_extra):
            self.wait(v, "lf", self.lf_total)
            self.wait(v, rdy[0], rdy[1])
            self.wait(v, "v", prod_ready)
            for nm, val in bo_extra:
                self.wait(v, nm, val)
            if br == 1:
                ins = v.tensor_scalar(self.vrow[0:1, 0:512], self.ninv_r[nslot][0:1, :],
                                      self.alpha[0:1, 0:1], None, ALU.mult)
            else:
                ins = v.tensor_copy(self.vrow[0:1, 0:512], self.ninv_r[nslot][0:1, :])
            self.inc(ins, "v")
        self.V(fvz)
        vwz = self.cnt["v"]
        gn6, gv6 = self.gQ[6]

        def ftz(t):
            self.wait(t, "v", vwz)
            self.wait(t, gn6, gv6)
            self.inc(t.matmul(self.Q[6][:], self.ones_row[:], self.vrow[0:1, 0:512],
                              start=True, stop=True), "t")
        self.T(ftz)
        twz = self.cnt["t"]

        def fvz2(v):
            self.wait(v, "t", twz)
            self.inc(v.tensor_copy(self.znb[:], self.Q[6][:]), "v")
        self.V(fvz2)
        self.gQ[6] = ("v", self.cnt["v"])
        znb_v = self.cnt["v"]

        for di in range(8):
            pb = di % 2
            P = self.Q[pb]
            gn, gv = self.gQ[pb]

            def ft(t, di=di, P=P, gn=gn, gv=gv):
                self.wait(t, uln, ulv)
                self.wait(t, "v", prod_ready)
                self.wait(t, gn, gv)
                for krc in range(4):
                    t.matmul(P[:], ubuf[:, krc * 1024 + di * 128: krc * 1024 + (di + 1) * 128],
                             self.prod2[:, krc * 512:(krc + 1) * 512],
                             start=(krc == 0), stop=False)
                self.wait(t, "lgc", self.lgc0)
                ins = t.matmul(P[:], b_sb[:, di * 128:(di + 1) * 128], self.eexp_bf[br][:],
                               start=False, stop=True)
                self.inc(ins, "t")
            self.T(ft)
            tw = self.cnt["t"]

            def fv(v, di=di, P=P, tw=tw):
                self.wait(v, "t", tw)
                self.wait(v, "v", znb_v)
                if add_prev is None:
                    ins = v.tensor_mul(out_dst[:, di * 512:(di + 1) * 512], P[:], self.znb[:])
                else:
                    v.tensor_mul(self.scr[:], P[:], self.znb[:])
                    ins = v.tensor_add(out_dst[:, di * 512:(di + 1) * 512], self.scr[:],
                                       add_prev[:, di * 512:(di + 1) * 512])
                self.inc(ins, "v")
            self.V(fv)
            self.gQ[pb] = ("v", self.cnt["v"])
        return self.cnt["v"]

    def _ffn_chunk(self, c, ln3v, rdy):
        col = c * CH
        slot = c % 2
        self.cur_slot = slot
        xcs = self.xc[slot][:]
        self.cur_rdy = rdy

        rdy_next = self._xc_load(c + 1, 1 - slot) if c + 1 < NCH else None

        kv = self._mm1024("wk_t", self.ln3c, self.kc, [("v", ln3v)])
        self._ar_free("big", "wk_t")
        qv = self._mm1024("wq_t", self.ln3c, self.qc, [("v", ln3v)])
        self._ar_free("big", "wq_t")

        iqv = self._mm1024("winv_t", self.qc, self.iqc, [("v", qv)])
        ikv = self._mm1024("winv_t", self.kc, self.ikc, [("v", kv)])
        if c == 0:
            self._exchange_tail()
        self._ar_free("big", "winv_t")

        ef = self._router(0, self.qc, ("v", qv))
        ei = self._router(1, self.iqc, ("v", iqv))
        self._ar_free("big", "rw1_t")

        pf = self._branch_core(0, self.qc, self.kc, "vf_r", "wf_r", qv, kv, ef, self.carry_f)
        self._ar_free("small", "vf_r")
        self._ar_free("small", "wf_r")
        of = self._branch_out(0, "uf_r", self.bf_b, pf, col, self.ofwd)
        self._ar_free("small", "uf_r")

        pi = self._branch_core(1, self.iqc, self.ikc, "wie_r", "vi_r", iqv, ikv, ei, self.carry_i)
        self._ar_free("small", "wie_r")
        self._ar_free("small", "vi_r")

        early_ln3 = rdy_next is not None and not (self.debug and c == 0)
        if early_ln3:
            ln3_vw = self._ln_sq(self.xc[1 - slot][:], rdy_next,
                                 fv_extra=(("v", pf),))

        ci = self._branch_out(1, "ui_r", self.bi_b, pi, col, self.cmb,
                              add_prev=self.ofwd)
        self._ar_free("small", "ui_r")

        if early_ln3:
            ln3_sw, ln3_tw = self._ln_stats(ln3_vw, (2, 3))

        fv_ = self._mm1024("wo_t", self.cmb, self.fout, [("v", ci)],
                           pairs=((0, 1), (6, 7)))
        self._ar_free("big", "wo_t")

        if c == 0 and self.debug:
            self.dbg("qc0", self.qc[:], (128, 8 * 512), BF16)
            self.dbg("ef0", self.eexp_bf[0][:], (8, 512), BF16)
            self.dbg("prod20", self.prod2[:], (128, 4 * 512), BF16)
            self.dbg("ofwd0", self.ofwd[:], (128, 8 * 512), BF16)
            self.dbg("fout0", self.fout[:], (128, 8 * 512), BF16)
            dbg_bar = (("st", self.cnt["st"]),)
        else:
            dbg_bar = ()

        if early_ln3:
            ln3v_next = self._ln_apply(self.xc[1 - slot][:], self.ln3c,
                                       ln3_sw, (2, 3))
        elif rdy_next is not None:
            ln3v_next, ln3_tw = self._ln_chunk(
                self.xc[1 - slot][:], rdy_next, self.ln3c, banks=(2, 3),
                fv_extra=(("v", ci),) + dbg_bar,
                fv3_extra=(("v", ci),) + dbg_bar)
        else:
            ln3v_next, ln3_tw = None, None

        def fvq(v, fv_=fv_, ln3_tw=ln3_tw):
            self.wait(v, "v", fv_)
            if ln3_tw is not None:
                self.wait(v, "t", ln3_tw)
            for j in range(8):
                ins = v.tensor_mul(self.x2bf2[:, j * 512:(j + 1) * 512],
                                   self.fout[:, j * 512:(j + 1) * 512],
                                   self.fout[:, j * 512:(j + 1) * 512])
            self.inc(ins, "v")
        self.V(fvq)
        vwq = self.cnt["v"]
        psA, psB = self.Q[4], self.Q[5]
        (gn0, gv0), (gn1, gv1) = self.gQ[4], self.gQ[5]

        def ft(t):
            self.wait(t, "v", vwq)
            self.wait(t, gn0, gv0)
            self.wait(t, gn1, gv1)
            for j in range(8):
                t.matmul(psA[:], self.ones_bf[:], self.fout[:, j * 512:(j + 1) * 512],
                         start=(j == 0), stop=(j == 7))
            for j in range(8):
                ins = t.matmul(psB[:], self.ones_bf[:], self.x2bf2[:, j * 512:(j + 1) * 512],
                               start=(j == 0), stop=(j == 7))
            self.inc(ins, "t")
        self.T(ft)
        tw = self.cnt["t"]

        def fsx(v):
            self.wait(v, "t", tw)
            v.tensor_scalar(self.mrow[0:1, 0:512], psA[0:1, :], 1.0 / D, None, ALU.mult)
            self.inc(v.tensor_scalar(self.vrow[0:1, 0:512], psB[0:1, :], 1.0 / D, None, ALU.mult), "v")
        self.V(fsx)
        sw = self._ln_rows(512)

        def ft2(t):
            self.wait(t, "s", sw)
            self.inc(t.matmul(psB[:], self.ones_row[:], self.rrow[0:1, 0:512],
                              start=True, stop=True), "t")
        self.T(ft2)
        tw2 = self.cnt["t"]

        def fv2(v):
            self.wait(v, "t", tw2)
            for j in range(8):
                xs = xcs[:, j * 512:(j + 1) * 512]
                v.scalar_tensor_tensor(self.scr[:], psA[:], -1.0 / D,
                                       self.fout[:, j * 512:(j + 1) * 512], ALU.mult, ALU.add)
                v.tensor_mul(self.scr[:], self.scr[:], psB[:])
                ins = v.tensor_add(xs, xs, self.scr[:])
            self.inc(ins, "v")
        self.V(fv2)
        self.gQ[4] = ("v", self.cnt["v"])
        self.gQ[5] = ("v", self.cnt["v"])
        vfin = self.cnt["v"]

        def fy(sy):
            self.wait(sy, "v", vfin)
            dst = self.dr["y"].rearrange("j p t -> p j t")[:, :, col:col + 512]
            self.inc(sy.dma_start(out=dst,
                                  in_=xcs.rearrange("p (j t) -> p j t", j=8)), "st", 16)
        self.Y(fy)
        self.g_xc[slot] = ("st", self.cnt["st"])
        return ln3v_next, rdy_next

    def _finish(self):
        stw = self.cnt["st"]

        def fy(sy):
            self.wait(sy, "st", stw)
        self.Y(fy)



def _prep_core_inputs(inputs, core):
    b, half = core // 2, core % 2
    x = np.asarray(inputs["L0_attn_inputs"])
    d = {}
    xt = np.ascontiguousarray(x[b, half * TOK:(half + 1) * TOK, :].T)
    d["x"] = np.ascontiguousarray(xt.reshape(8, 128, TOK))

    ninv = (1.0 / (np.arange(TOK, dtype=np.float64) + 1 + half * TOK)).astype(np.float32)
    d["ninv"] = ninv.reshape(1, TOK)
    mskb = np.zeros((2, 1), np.float32)
    if half == 1:
        mskb[0, 0] = 1.0
    d["mskb"] = mskb
    return d


def _prep_shared_inputs(inputs):
    d = {}
    Wqkv = np.asarray(inputs["L0_attn_qkv_kernel"])
    bias = np.asarray(inputs["L0_attn_qkv_bias"]).reshape(3072)
    perm = []
    for a in range(16):
        perm += list(range(a * 192, a * 192 + 128))
    for a in range(16):
        perm += list(range(a * 192 + 128, a * 192 + 192))
    perm = np.array(perm)
    Wp = np.ascontiguousarray(Wqkv[:, perm])
    bp = np.ascontiguousarray(bias[perm])
    d["wqkv"] = np.ascontiguousarray(Wp.reshape(8, 128, 3072))
    d["biasp"] = np.ascontiguousarray(bp.reshape(24, 128).T)

    def bigT(w):
        return np.ascontiguousarray(np.ascontiguousarray(np.asarray(w).T).reshape(8, 128, 1024))
    d["wq_t"] = bigT(inputs["L0_ffn_W_Q"])
    d["wk_t"] = bigT(inputs["L0_ffn_W_K"])
    d["winv_t"] = bigT(inputs["L0_ffn_W_inv"])
    d["rw1_t"] = bigT(inputs["L0_ffn_router_w1"])
    d["wo_t"] = bigT(inputs["L0_ffn_W_O"])
    rw2 = np.asarray(inputs["L0_ffn_router_w2"])
    d["rw2_t"] = np.ascontiguousarray(np.ascontiguousarray(rw2.T).reshape(8, 128, 8))

    def vr(w):
        a = np.transpose(np.asarray(w), (1, 0, 2)).reshape(1024, 512)
        return np.ascontiguousarray(np.ascontiguousarray(a).reshape(8, 128, 512))

    def ur(w):
        a = np.transpose(np.asarray(w), (0, 2, 1)).reshape(512, 1024)
        return np.ascontiguousarray(np.ascontiguousarray(a).reshape(4, 128, 1024))
    d["vf_r"] = vr(inputs["L0_ffn_V_fwd"])
    d["wf_r"] = vr(inputs["L0_ffn_W_fwd"])
    d["wie_r"] = vr(inputs["L0_ffn_W_inv_exp"])
    d["vi_r"] = vr(inputs["L0_ffn_V_inv"])
    d["uf_r"] = ur(inputs["L0_ffn_U_fwd"])
    d["ui_r"] = ur(inputs["L0_ffn_U_inv"])
    d["b_fwd"] = np.ascontiguousarray(np.asarray(inputs["L0_ffn_b_fwd"], np.float32))
    d["b_inv"] = np.ascontiguousarray(np.asarray(inputs["L0_ffn_b_inv"], np.float32))
    rb1 = np.asarray(inputs["L0_ffn_router_b1"]).reshape(8, 128)
    d["rb1"] = np.ascontiguousarray(rb1.T)
    d["b2"] = np.ascontiguousarray(np.asarray(inputs["L0_ffn_router_b2"], np.float32).reshape(8, 1))
    d["eb"] = np.ascontiguousarray(np.asarray(inputs["L0_ffn_expert_bias"], np.float32).reshape(8, 1))
    d["alpha"] = np.asarray(inputs["L0_ffn_alpha_bi"], np.float32).reshape(1, 1)

    d["ident"] = np.eye(128, dtype=np.float32)
    mm = np.zeros((128, MW), dtype=np.float32)
    for i in range(128):
        mm[i, i + MC:] = 1.0
    d["mmask"] = mm.astype(ml_dtypes.bfloat16)
    sel = np.zeros((8, 512), np.float32)
    for krc in range(4):
        for p in range(128):
            sel[2 * krc + p // 64, krc * 128 + p] = 1.0
    d["sel"] = sel.astype(ml_dtypes.bfloat16)
    return d


def kernel(**inputs):
    n_cores = 8
    shared = _prep_shared_inputs(inputs)
    in_maps = []
    for c in range(n_cores):
        m = dict(shared)
        m.update(_prep_core_inputs(inputs, c))
        in_maps.append(m)
    b = Builder(n_cores=n_cores, debug=False)
    nc = b.build()
    res = run_bass_kernel_spmd(nc, in_maps, core_ids=list(range(n_cores)))
    out = np.zeros((4, 4096, 1024), np.float32)
    for c in range(n_cores):
        y = np.asarray(res.results[c]["y"])
        bb, half = c // 2, c % 2
        out[bb, half * TOK:(half + 1) * TOK, :] = y.reshape(1024, TOK).T
    return out
